# version 11
# baseline (speedup 1.0000x reference)
import contextlib
import numpy as np
import ml_dtypes
import concourse.bass as bass
import concourse.mybir as mybir
from concourse.bass_utils import run_bass_kernel_spmd

F32 = mybir.dt.float32
BF16 = mybir.dt.bfloat16
I32 = mybir.dt.int32
U8 = mybir.dt.uint8
U16 = mybir.dt.uint16
FP8 = mybir.dt.float8e4
OP = mybir.AluOpType
AF = mybir.ActivationFunctionType
AX = mybir.AxisListType.X

D = 2048
NH = 16
FN = 16384
NB = 64
EPS = 1e-5
ISQ = float(1.0 / np.sqrt(128.0))
NEGB = -1.0e9
LUTS = 64.0

STAGE = 3
DEBUG = False
SUB = 99

ENGS = ("pe", "act", "dve", "sp", "pool")
DMASEMS = ("x", "pan0", "pan1", "pk0", "pk1", "cst",
           "txn", "tv0", "tv1", "ta", "tn", "th1", "trc", "st",
           "wg00", "wg01", "wg10", "wg11",
           "sw0", "sw1", "sw2", "sw3", "sr0", "sr1", "sr2", "sr3",
           "sdw0", "sdw1", "sdr0", "sdr1")
SEMS = ENGS[:3] + DMASEMS
SZ = {F32: 4, BF16: 2, I32: 4, U8: 1, U16: 2, FP8: 1}


class Em:
    def __init__(self):
        self.q = {e: [] for e in ENGS}
        self.n = {s: 0 for s in SEMS}
        self.lastw = {e: {d: 0 for d in SEMS} for e in ENGS}

    def op(self, eng, builder, waits=None, dma=False, sem=None,
           noinc=False):
        if sem is None:
            sem = eng
            assert not dma or eng not in ("sp", "pool")
        w2 = []
        if waits:
            for d, v in sorted(waits.items()):
                if v is None or v <= 0:
                    continue
                if d == sem and not dma:
                    assert v <= self.n[sem], (sem, v, self.n[sem])
                if v > self.lastw[eng][d]:
                    w2.append((d, v))
                    self.lastw[eng][d] = v
        if noinc:
            self.q[eng].append((w2, builder, 0, sem))
            return None
        inc = 16 if dma else 1
        self.q[eng].append((w2, builder, inc, sem))
        self.n[sem] += inc
        return self.n[sem]


def mw(*dicts):
    out = {}
    for dd in dicts:
        if not dd:
            continue
        for k, v in dd.items():
            if v is None:
                continue
            out[k] = max(out.get(k, 0), v)
    return out


def build_nc(stage=None, debug=None, sub=None):
    if stage is None:
        stage = STAGE
    if debug is None:
        debug = DEBUG
    if sub is None:
        sub = SUB
    nc = bass.Bass("TRN2", target_bir_lowering=False)

    x_hbm = nc.dram_tensor("x", [2048, D], F32, kind="ExternalInput").ap()
    wq = nc.dram_tensor("wq", [D, D], F32, kind="ExternalInput").ap()
    wk = nc.dram_tensor("wk", [D, D], F32, kind="ExternalInput").ap()
    wv = nc.dram_tensor("wv", [D, D], F32, kind="ExternalInput").ap()
    wo = nc.dram_tensor("wo", [D, D], F32, kind="ExternalInput").ap()
    gT = nc.dram_tensor("gT", [128, 16], F32, kind="ExternalInput").ap()
    bT = nc.dram_tensor("bT", [128, 16], F32, kind="ExternalInput").ap()
    bqT = nc.dram_tensor("bqT", [128, 16], F32, kind="ExternalInput").ap()
    bkT = nc.dram_tensor("bkT", [128, 16], F32, kind="ExternalInput").ap()
    bvT = nc.dram_tensor("bvT", [128, 16], F32, kind="ExternalInput").ap()
    boT = nc.dram_tensor("boT", [128, 16], F32, kind="ExternalInput").ap()
    maskh = nc.dram_tensor("mask", [128, 128], F32, kind="ExternalInput").ap()
    gpk = nc.dram_tensor("gpk", [FN, 1024], I32, kind="ExternalInput").ap()
    upk = nc.dram_tensor("upk", [FN, 1024], I32, kind="ExternalInput").ap()
    dpk = nc.dram_tensor("dpk", [D, 8192], I32, kind="ExternalInput").ap()
    gsc = nc.dram_tensor("gsc", [FN, 64], F32, kind="ExternalInput").ap()
    usc = nc.dram_tensor("usc", [FN, 64], F32, kind="ExternalInput").ap()
    dsc = nc.dram_tensor("dsc", [D, 512], F32, kind="ExternalInput").ap()
    y_hbm = nc.dram_tensor("y", [2048, D], F32, kind="ExternalOutput").ap()
    wspg = nc.dram_tensor("wspg", [NB, 2, 128, 4096], BF16,
                          kind="Internal").ap()
    wspd = nc.dram_tensor("wspd", [16, 8, 128, 2048], BF16,
                          kind="Internal").ap()
    dbg = {}
    if debug:
        dbg["A"] = nc.dram_tensor("dbgA", [128, 32768], BF16,
                                  kind="ExternalOutput").ap()
        dbg["B"] = nc.dram_tensor("dbgB", [128, 32768], BF16,
                                  kind="ExternalOutput").ap()
    em = Em()
    E = em.op

    with contextlib.ExitStack() as stack:
        ec = stack.enter_context
        regA = ec(nc.sbuf_tensor("regA", [128, 65536], U8))
        regB = ec(nc.sbuf_tensor("regB", [128, 65536], U8))
        regC = ec(nc.sbuf_tensor("regC", [128, 55296], U8))
        regD = ec(nc.sbuf_tensor("regD", [128, 26112], U8))
        pp = [ec(nc.psum_tensor(f"pp{i}", [128, 512], F32)) for i in range(8)]
        sems = {s: ec(nc.semaphore("sem_" + s)) for s in SEMS}
        block = ec(nc.Block())

        def view(reg, off, dt_, n):
            return reg.ap()[:, off:off + n * SZ[dt_]].bitcast(dt_)

        KT = view(regA, 0, BF16, 32768).rearrange(
            "p (s c t) -> p s c t", s=4, c=16, t=512)
        H1P = view(regA, 0, BF16, 32768).rearrange(
            "p (s j d) -> p s j d", s=4, j=4, d=2048)
        ACC = view(regA, 0, F32, 16384).rearrange(
            "p (c t) -> p c t", c=16, t=1024)
        VN = view(regB, 0, BF16, 32768).rearrange(
            "p (c d) -> p c d", c=16, d=2048)
        H1T = view(regB, 0, BF16, 32768).rearrange(
            "p (c t) -> p c t", c=16, t=2048)
        XNH = view(regC, 0, BF16, 16384).rearrange(
            "p (c t) -> p c t", c=16, t=1024)
        XNS = view(regC, 0, BF16, 8192).rearrange(
            "p (c t) -> p c t", c=16, t=512)
        ATTNT = view(regC, 0, BF16, 8192).rearrange(
            "p (c t) -> p c t", c=16, t=512)
        QTB = view(regC, 16384, BF16, 1024).rearrange(
            "p (a t) -> p a t", a=2, t=512)
        SROW = view(regC, 18432, F32, 2048)
        AROW = view(regC, 26624, BF16, 2048)
        ATB = view(regC, 30720, BF16, 4096).rearrange(
            "p (a c t) -> p a c t", a=2, c=16, t=128)
        AVT = view(regC, 38912, BF16, 8192).rearrange(
            "p (c t) -> p c t", c=16, t=512)
        ATTNN = view(regC, 38912, BF16, 8192).rearrange(
            "p (j d) -> p j d", j=4, d=2048)
        INTER = view(regC, 0, BF16, 8192).rearrange(
            "p (c t) -> p c t", c=8, t=1024)
        WGT = view(regC, 16384, BF16, 16384).rearrange(
            "p (a c n) -> p a c n", a=4, c=16, n=256)
        Z3 = view(regC, 0, F32, 2048)
        OUTS = view(regC, 8192, F32, 2048)
        H1R = view(regC, 16384, BF16, 2048)
        FFNR = view(regC, 20480, BF16, 2048)
        G1B = view(regC, 49152, BF16, 1024).rearrange(
            "p (a t) -> p a t", a=2, t=512)
        SCB = view(regC, 51200, F32, 128).rearrange(
            "p (a c) -> p a c", a=2, c=64)
        XB = view(regD, 0, F32, 2048)
        XNB = view(regD, 8192, BF16, 2048)
        PANEL = view(regD, 12288, BF16, 4096).rearrange(
            "p (a c n) -> p a c n", a=2, c=16, n=128)
        VTS = view(regD, 20480, BF16, 2048).rearrange(
            "p (a t) -> p a t", a=2, t=1024)
        PKU = view(regD, 0, U8, 2048).rearrange(
            "p (a k) -> p a k", a=2, k=1024)
        ABU = view(regD, 2048, U16, 3072).rearrange(
            "p (s a k) -> p s a k", s=2, a=3, k=512)
        WNAT = view(regD, 8192, BF16, 4096).rearrange(
            "p (s a k) -> p s a k", s=2, a=2, k=1024)
        WDT = view(regD, 16384, BF16, 4096).rearrange(
            "p (a c h) -> p a c h", a=2, c=8, h=256)
        ACCBF = view(regD, 0, BF16, 2048).rearrange(
            "p (c t) -> p c t", c=16, t=128)
        AROW2 = view(regD, 20480, BF16, 2048)
        C0 = 24576
        GTT = view(regD, C0, F32, 16)
        BTT = view(regD, C0 + 64, F32, 16)
        BQT = view(regD, C0 + 128, F32, 16)
        BKT = view(regD, C0 + 192, F32, 16)
        BVT = view(regD, C0 + 256, F32, 16)
        BOT = view(regD, C0 + 320, F32, 16)
        MSK = view(regD, C0 + 384, F32, 128)
        ST0 = C0 + 1024

        def stat(i):
            return view(regD, ST0 + 4 * i, F32, 1)

        (S_SUM, S_NM, S_LNV, S_RSTD, S_NMRS, S_SSQ, S_NMX, S_RS, S_RCP,
         S_EPS, S_NMX1, S_RS1, S_RCP1, S_CM0, S_CM1, S_CM2, S_CM3) = \
            range(17)
        ev_eps = E("dve", lambda e: e.memset(stat(S_EPS), EPS))

        ev_consts = 0
        for apd, src in ((GTT, gT), (BTT, bT), (BQT, bqT), (BKT, bkT),
                         (BVT, bvT), (BOT, boT), (MSK, maskh)):
            ev_consts = E("pool", lambda e, a=apd, s=src:
                          e.dma_start(out=a, in_=s), dma=True, sem="cst")

        def emit_ln(ev_x_pool, pre_dve, junkap, outap, junk_waits=None,
                    out_waits=None):
            ev1 = E("dve", lambda e: e.tensor_reduce(
                out=stat(S_SUM), in_=XB, axis=AX, op=OP.add),
                waits=mw({"x": ev_x_pool}, pre_dve))
            ev2 = E("dve", lambda e: e.tensor_scalar(
                out=stat(S_NM), in0=stat(S_SUM), scalar1=-1.0 / D,
                scalar2=None, op0=OP.mult), waits={"dve": ev1})
            ev3 = E("act", lambda e, j=junkap: e.activation(
                out=j, in_=XB, func=AF.Square, bias=stat(S_NM),
                accum_out=stat(S_SSQ)),
                waits=mw({"dve": ev2, "x": ev_x_pool}, junk_waits))
            ev4 = E("act", lambda e: e.activation(
                out=stat(S_LNV), in_=stat(S_SSQ), func=AF.Ln,
                bias=stat(S_EPS), scale=1.0 / D),
                waits={"act": ev3, "dve": ev_eps})
            ev5 = E("act", lambda e: e.activation(
                out=stat(S_RSTD), in_=stat(S_LNV), func=AF.Exp,
                scale=-0.5), waits={"act": ev4})
            ev6 = E("dve", lambda e: e.tensor_tensor(
                out=stat(S_NMRS), in0=stat(S_NM), in1=stat(S_RSTD),
                op=OP.mult), waits={"act": ev5, "dve": ev2})
            ev7 = E("act", lambda e, o=outap: e.activation(
                out=o, in_=XB, func=AF.Identity, scale=stat(S_RSTD),
                bias=stat(S_NMRS)), waits=mw({"dve": ev6}, out_waits))
            return ev7, ev1

        ev_xb_free = {}
        ev_panel_free = [{}, {}]
        ev_vts_free = [{}, {}]
        ev_psum_free = [{} for _ in range(8)]
        ev_xnt_free = {}
        ev_xnb_free = {}
        ev_h1t = 0

        for th in range(2):
            ev_gb = 0
            for j8 in range(8):
                jt = th * 8 + j8
                ev_x = E("pool", lambda e, j=jt: e.dma_start(
                    out=XB, in_=x_hbm[j * 128:(j + 1) * 128, :]),
                    waits=ev_xb_free, dma=True, sem="x")
                ev_ln, ev_lnr = emit_ln(ev_x, None, XNB, XNB,
                                        junk_waits=ev_xnb_free)
                ev_xb_free = {"act": ev_ln, "dve": ev_lnr}
                ev_tp = E("sp", lambda e, j=j8: e.dma_start_transpose(
                    out=XNH[:, :, j * 128:(j + 1) * 128], in_=XNB),
                    waits=mw({"act": ev_ln}, ev_xnt_free), dma=True,
                    sem="txn")
                ev_xnb_free = {"txn": ev_tp}
                for dc in range(16):
                    ev_gb = E("dve", lambda e, c=dc, j=j8: e.tensor_scalar(
                        out=XNH[:, c, j * 128:(j + 1) * 128],
                        in0=XNH[:, c, j * 128:(j + 1) * 128],
                        scalar1=GTT[:, c:c + 1], scalar2=BTT[:, c:c + 1],
                        op0=OP.mult, op1=OP.add),
                        waits={"txn": ev_tp, "cst": ev_consts})
            for wi, whbm, bap in ((0, wv, BVT), (1, wk, BKT)):
                for c in range(16):
                    slot = c % 2
                    ev_pl = E("pool", lambda e, s=slot, cc=c, W=whbm:
                        e.dma_start(
                            out=PANEL[:, s, :, :],
                            in_=W.rearrange("(ci p) n -> p ci n", p=128)
                                [:, :, cc * 128:(cc + 1) * 128]),
                        waits=ev_panel_free[slot], dma=True,
                        sem="pan%d" % slot)
                    eve = None
                    for tg in range(2):
                        bank = (c * 2 + tg) % 4
                        for dc in range(16):
                            evm = E("pe", lambda e, s=slot, d=dc, t=tg,
                                    b=bank: e.matmul(
                                    out=pp[b][:], lhsT=PANEL[:, s, d, :],
                                    rhs=XNH[:, d, t * 512:(t + 1) * 512],
                                    start=(d == 0), stop=(d == 15)),
                                waits=mw({"pan%d" % slot: ev_pl,
                                          "dve": ev_gb},
                                         ev_psum_free[bank]))
                        ev_panel_free[slot] = {"pe": evm}
                        ev_xnt_free = {"pe": evm}
                        if wi == 1:
                            sup = th * 2 + tg
                            eve = E("act", lambda e, b=bank, s=sup, cc=c,
                                    ba=bap: e.activation(
                                    out=KT[:, s, cc, :], in_=pp[b][:],
                                    func=AF.Identity,
                                    bias=ba[:, cc:cc + 1]),
                                waits={"pe": evm, "cst": ev_consts})
                        else:
                            eve = E("act", lambda e, b=bank, t=tg, cc=c,
                                    s=slot, ba=bap: e.activation(
                                    out=VTS[:, s, t * 512:(t + 1) * 512],
                                    in_=pp[b][:], func=AF.Identity,
                                    bias=ba[:, cc:cc + 1]),
                                waits=mw({"pe": evm, "cst": ev_consts},
                                         ev_vts_free[slot]))
                        ev_psum_free[bank] = {"act": eve}
                    if wi == 0:
                        evv = E("sp", lambda e, s=c % 2, cc=c, t=th:
                            e.dma_start_transpose(
                                out=VN[:, t * 8:(t + 1) * 8,
                                       cc * 128:(cc + 1) * 128],
                                in_=VTS[:, s, :]),
                            waits={"act": eve}, dma=True,
                            sem="tv%d" % (c % 2))
                        ev_vts_free[c % 2] = {"tv%d" % (c % 2): evv}

        if debug and stage == 1:
            E("sp", lambda e: e.dma_start(
                out=dbg["A"], in_=view(regA, 0, BF16, 32768)),
                waits={"st": em.n["st"], "act": em.n["act"], "dve": em.n["dve"],
                       "pe": em.n["pe"]}, dma=True, sem="st")
            E("sp", lambda e: e.dma_start(
                out=dbg["B"], in_=view(regB, 0, BF16, 32768)),
                waits={"st": em.n["st"], "tv0": em.n["tv0"], "tv1": em.n["tv1"]},
                dma=True, sem="st")
        dbgw = {"st": em.n["st"]} if debug else {}

        ev_tv0 = em.n["tv0"]
        ev_tv1 = em.n["tv1"]
        ev_h1_act = 0
        if stage >= 2:
            ev_qt_free = [{}, {}]
            ev_qpanel_free = [dict(ev_panel_free[0]), dict(ev_panel_free[1])]
            ev_xns_free = dict(ev_xnt_free)
            ev_avt_free = {}
            ev_srow_free = [{}, {}]
            ev_arow_free = [{}, {"tv0": ev_tv0, "tv1": ev_tv1}]
            ev_atb_free = [{}, {}]

            for g in (3, 2, 1, 0):
                for j in range(4):
                    jt = 4 * g + j
                    ev_x = E("pool", lambda e, t=jt: e.dma_start(
                        out=XB, in_=x_hbm[t * 128:(t + 1) * 128, :]),
                        waits=ev_xb_free, dma=True, sem="x")
                    ev_ln, ev_lnr = emit_ln(ev_x, None, XNB, XNB,
                                            junk_waits=ev_xnb_free)
                    ev_xb_free = {"act": ev_ln, "dve": ev_lnr}
                    ev_tp = E("sp", lambda e, j_=j: e.dma_start_transpose(
                        out=XNS[:, :, j_ * 128:(j_ + 1) * 128], in_=XNB),
                        waits=mw({"act": ev_ln}, ev_xns_free), dma=True,
                        sem="txn")
                    ev_xnb_free = {"txn": ev_tp}
                    for dc in range(16):
                        ev_gb = E("dve", lambda e, c=dc, j_=j:
                            e.tensor_scalar(
                                out=XNS[:, c, j_ * 128:(j_ + 1) * 128],
                                in0=XNS[:, c, j_ * 128:(j_ + 1) * 128],
                                scalar1=GTT[:, c:c + 1],
                                scalar2=BTT[:, c:c + 1],
                                op0=OP.mult, op1=OP.add),
                            waits={"txn": ev_tp, "cst": ev_consts})
                nk = 4 * (g + 1)
                ev_last_S = 0
                ev_exps = [0, 0]
                prev_tail = None
                ev_srow_free[1] = mw(ev_srow_free[1], ev_xb_free)
                for h in range(16) if sub >= 2 else []:
                    slot = h % 2
                    ev_pl = E("pool", lambda e, s=slot, hh=h: e.dma_start(
                        out=PANEL[:, s, :, :],
                        in_=wq.rearrange("(ci p) n -> p ci n", p=128)
                            [:, :, hh * 128:(hh + 1) * 128]),
                        waits=ev_qpanel_free[slot], dma=True,
                        sem="pan%d" % slot)
                    qb = h % 4
                    for dc in range(16):
                        evm = E("pe", lambda e, s=slot, d=dc, b=qb:
                            e.matmul(
                                out=pp[b][:], lhsT=PANEL[:, s, d, :],
                                rhs=XNS[:, d, :],
                                start=(d == 0), stop=(d == 15)),
                            waits=mw({"pan%d" % slot: ev_pl,
                                      "dve": ev_gb},
                                     ev_psum_free[qb]))
                    ev_qpanel_free[slot] = {"pe": evm}
                    ev_qe = E("act", lambda e, b=qb, s=slot, hh=h:
                        e.activation(out=QTB[:, s, :], in_=pp[b][:],
                                     func=AF.Identity,
                                     bias=BQT[:, hh:hh + 1]),
                        waits=mw({"pe": evm, "cst": ev_consts},
                                 ev_qt_free[slot]))
                    ev_psum_free[qb] = {"act": ev_qe}
                    def emit_tail(pv):
                        qt_, nkq_, klen_, ar_, hh_ = pv
                        par_ = qt_ % 2
                        ev_rcp = E("dve", lambda e, pa=par_: e.reciprocal(
                            out=stat(S_RCP + 4 * pa), in_=stat(S_RS + 4 * pa)),
                            waits={"act": ev_exps[par_]})
                        ev_an = E("dve", lambda e, kl=klen_, ar=ar_, pa=par_:
                            e.tensor_scalar(
                                out=ar[:, 0:kl], in0=ar[:, 0:kl],
                                scalar1=stat(S_RCP + 4 * pa), scalar2=None,
                                op0=OP.mult),
                            waits={"dve": ev_rcp})
                        ev_ta = E("sp", lambda e, kl=klen_, nq=nkq_,
                                  a=par_, ar=ar_:
                            e.dma_start_transpose(
                                out=ATB[:, a, 0:nq, :],
                                in_=ar[:, 0:kl]),
                            waits=mw({"dve": ev_an}, ev_atb_free[par_]),
                            dma=True, sem="ta")
                        ev_arow_free[par_] = {"ta": ev_ta}
                        ab = 6 + par_
                        evav = 0
                        for kc in range(nkq_):
                            evav = E("pe", lambda e, k=kc, hh=hh_, b=ab,
                                     a=par_, nq=nkq_: e.matmul(
                                    out=pp[b][:, 0:128],
                                    lhsT=VN[:, k, hh * 128:(hh + 1) * 128],
                                    rhs=ATB[:, a, k, :],
                                    start=(k == 0), stop=(k == nq - 1)),
                                waits=mw({"ta": ev_ta, "tv0": ev_tv0,
                                          "tv1": ev_tv1},
                                         ev_psum_free[ab]))
                        ev_atb_free[par_] = {"pe": evav}
                        eva2 = E("dve", lambda e, b=ab, hh=hh_, q=qt_:
                            e.tensor_copy(
                                out=AVT[:, hh, q * 128:(q + 1) * 128],
                                in_=pp[b][:, 0:128]),
                            waits=mw({"pe": evav}, ev_avt_free))
                        ev_psum_free[ab] = {"dve": eva2}
                        return eva2

                    for qt in range(4):
                        nkq = g * 4 + qt + 1
                        klen = nkq * 128
                        par = qt % 2
                        srow = SROW if par == 0 else XB
                        ev_cm = 0
                        for ks in range(g + 1):
                            w = 512 if ks < g else (qt + 1) * 128
                            sb = 4 + (ks % 2)
                            evs = E("pe", lambda e, b=sb, s=slot, q=qt,
                                    k=ks, ww=w, hh=h: e.matmul(
                                    out=pp[b][:, 0:ww],
                                    lhsT=QTB[:, s, q * 128:(q + 1) * 128],
                                    rhs=KT[:, k, hh, 0:ww],
                                    start=True, stop=True),
                                waits=mw({"act": ev_qe},
                                         ev_psum_free[sb]))
                            ev_last_S = evs
                            evd = 0
                            if ks < g:
                                if ks % 2 == 0:
                                    eve = E("act", lambda e, b=sb, k=ks,
                                            ww=w, sr=srow: e.activation(
                                            out=sr[:, k * 512:
                                                   k * 512 + ww],
                                            in_=pp[b][:, 0:ww],
                                            func=AF.Identity, scale=ISQ),
                                        waits=mw({"pe": evs},
                                                 ev_srow_free[par]))
                                    ev_psum_free[sb] = {"act": eve}
                                else:
                                    eve = E("dve", lambda e, b=sb, k=ks,
                                            ww=w, sr=srow: e.tensor_scalar(
                                            out=sr[:, k * 512:
                                                   k * 512 + ww],
                                            in0=pp[b][:, 0:ww],
                                            scalar1=ISQ, scalar2=None,
                                            op0=OP.mult),
                                        waits=mw({"pe": evs},
                                                 ev_srow_free[par]))
                                    ev_psum_free[sb] = {"dve": eve}
                                evd = 0
                                cmw = {"act": eve} if ks % 2 == 0 \
                                    else {"dve": eve}
                            else:
                                if qt > 0:
                                    E("act", lambda e, b=sb, q=qt,
                                      gg=g, sr=srow: e.activation(
                                        out=sr[:, gg * 512:
                                               gg * 512 + q * 128],
                                        in_=pp[b][:, 0:q * 128],
                                        func=AF.Identity, scale=ISQ),
                                      waits=mw({"pe": evs},
                                               ev_srow_free[par]))
                                eve = E("act", lambda e, b=sb, q=qt,
                                        gg=g, sr=srow: e.activation(
                                        out=sr[:, gg * 512 + q * 128:
                                               gg * 512 + (q + 1) * 128],
                                        in_=pp[b][:,
                                                  q * 128:(q + 1) * 128],
                                        func=AF.Identity, scale=ISQ),
                                    waits=mw({"pe": evs},
                                             ev_srow_free[par]))
                                evd = E("dve", lambda e, q=qt, gg=g,
                                        sr=srow:
                                    e.tensor_tensor(
                                        out=sr[:, gg * 512 + q * 128:
                                               gg * 512 + (q + 1) * 128],
                                        in0=sr[:, gg * 512 + q * 128:
                                               gg * 512 + (q + 1) * 128],
                                        in1=MSK, op=OP.add),
                                    waits={"act": eve, "cst": ev_consts})
                                ev_psum_free[sb] = {"act": eve}
                                cmw = {"act": eve, "dve": evd}
                            ev_cm = E("dve", lambda e, k=ks, gg=g,
                                      q=qt, sr=srow, ww=w:
                                e.tensor_reduce(
                                    out=stat(S_CM0 + k),
                                    in_=sr[:, k * 512:k * 512 + ww],
                                    axis=AX, op=OP.max),
                                waits=cmw)
                        evmx = E("dve", lambda e, n_=g + 1, pa=par:
                            e.tensor_reduce(
                            out=stat(S_NMX + 4 * pa),
                            in_=view(regD, ST0 + 4 * S_CM0, F32, n_),
                            axis=AX, op=OP.max, negate=True),
                            waits={"dve": ev_cm})
                        arow = AROW if par == 0 else AROW2
                        ev_exps[par] = E("act", lambda e, kl=klen, ar=arow,
                                         pa=par, sr=srow: e.activation(
                            out=ar[:, 0:kl], in_=sr[:, 0:kl],
                            func=AF.Exp, bias=stat(S_NMX + 4 * pa),
                            accum_out=stat(S_RS + 4 * pa)),
                            waits=mw({"dve": evmx}, ev_arow_free[par]))
                        ev_srow_free[par] = {"act": ev_exps[par],
                                             "dve": evmx}
                        if prev_tail is not None:
                            ev_ave = emit_tail(prev_tail)
                        prev_tail = (qt, nkq, klen, arow, h)
                    ev_qt_free[slot] = {"pe": ev_last_S}
                ev_ave = emit_tail(prev_tail)
                prev_tail = None
                ev_avt_all = ev_ave
                ev_xb_free = mw(ev_xb_free, ev_srow_free[1])
                ev_srow_free[1] = {}

                dbg_st = {}
                ev_avt_free = {}
                ev_oe = None
                for c in range(16):
                    slot = c % 2
                    ev_pl = E("pool", lambda e, s=slot, cc=c: e.dma_start(
                        out=PANEL[:, s, :, :],
                        in_=wo.rearrange("(ci p) n -> p ci n", p=128)
                            [:, :, cc * 128:(cc + 1) * 128]),
                        waits=ev_qpanel_free[slot], dma=True,
                        sem="pan%d" % slot)
                    ob = c % 4
                    for dc in range(16):
                        evm = E("pe", lambda e, s=slot, d=dc, b=ob:
                            e.matmul(
                                out=pp[b][:], lhsT=PANEL[:, s, d, :],
                                rhs=AVT[:, d, :],
                                start=(d == 0), stop=(d == 15)),
                            waits=mw({"pan%d" % slot: ev_pl,
                                      "dve": ev_avt_all},
                                     ev_psum_free[ob]))
                    ev_qpanel_free[slot] = {"pe": evm}
                    ev_oe = E("act", lambda e, b=ob, cc=c: e.activation(
                        out=ATTNT[:, cc, :], in_=pp[b][:],
                        func=AF.Identity, bias=BOT[:, cc:cc + 1]),
                        waits=mw({"pe": evm, "cst": ev_consts}, dbg_st))
                    ev_psum_free[ob] = {"act": ev_oe}
                ev_o_pe = em.n["pe"]
                if sub < 8:
                    continue
                ev_attnN = 0
                for c in range(16):
                    ev_attnN = E("sp", lambda e, cc=c:
                        e.dma_start_transpose(
                            out=ATTNN[:, :, cc * 128:(cc + 1) * 128],
                            in_=ATTNT[:, cc, :]),
                        waits=mw({"act": ev_oe, "pe": ev_o_pe}, dbg_st),
                        dma=True, sem="tn")
                for j in range(4):
                    jt = 4 * g + j
                    ev_x = E("pool", lambda e, t=jt: e.dma_start(
                        out=XB, in_=x_hbm[t * 128:(t + 1) * 128, :]),
                        waits=ev_xb_free, dma=True, sem="x")
                    ev_z = E("dve", lambda e, j_=j: e.tensor_tensor(
                        out=XB, in0=XB, in1=ATTNN[:, j_, :], op=OP.add),
                        waits={"x": ev_x, "tn": ev_attnN})
                    h1ap = H1P[:, g, j, :].rearrange(
                        "p (b h s) -> p b s h", b=8, h=2, s=128)
                    ev_ln, ev_lnr = emit_ln(
                        ev_x, {"dve": ev_z}, XNB, h1ap,
                        junk_waits=ev_xnb_free,
                        out_waits=mw({"pe": ev_last_S}, dbgw))
                    ev_xb_free = {"act": ev_ln, "dve": ev_lnr}
                    ev_xnb_free = {"act": ev_ln}
                    ev_h1_act = ev_ln
                ev_xns_free = mw({"tn": em.n["tn"], "pe": em.n["pe"]},
                                 {"st": em.n["st"]} if debug else {})
                ev_avt_free = mw({"dve": em.n["dve"], "tn": em.n["tn"]},
                                 {"st": em.n["st"]} if debug else {})

        if debug and stage == 2:
            E("sp", lambda e: e.dma_start(
                out=dbg["A"], in_=view(regA, 0, BF16, 32768)),
                waits={"st": em.n["st"], "act": em.n["act"],
                       "dve": em.n["dve"], "pe": em.n["pe"]},
                dma=True, sem="st")
            E("sp", lambda e: e.dma_start(
                out=dbg["B"], in_=view(regB, 0, BF16, 32768)),
                waits={"st": em.n["st"]}, dma=True, sem="st")

        if stage >= 3:
            for jt in range(16):
                g, j = jt // 4, jt % 4
                ev_h1t = E("sp", lambda e, g_=g, j_=j, t=jt:
                    e.dma_start_transpose(
                        out=H1T[:, :, t * 128:(t + 1) * 128],
                        in_=H1P[:, g_, j_, :]),
                    waits={"act": ev_h1_act, "pe": em.n["pe"],
                           "dve": em.n["dve"]}, dma=True, sem="th1")

            def emit_dequant(pk_hbm, sc_hbm, r0, p0, s0, slot, out_e, out_o,
                             pk_free, ab_free, wnat_free, xbar_free,
                             pkw=1024):
                nsc = pkw // 16
                ev_pk = E("pool", lambda e: e.dma_start(
                    out=PKU[:, slot, 0:pkw],
                    in_=pk_hbm[r0:r0 + 128, p0:p0 + pkw]),
                    waits=pk_free, dma=True, sem="pk%d" % slot)
                ev_sc = E("pool", lambda e: e.dma_start(
                    out=SCB[:, slot, 0:nsc],
                    in_=sc_hbm[r0:r0 + 128, s0:s0 + nsc]),
                    waits=pk_free, dma=True, sem="pk%d" % slot)
                x16 = PKU[:, slot, 0:pkw].bitcast(U16)
                pw2 = pkw // 2
                e1 = E("dve", lambda e: e.tensor_scalar(
                    out=ABU[:, slot, 0, 0:pw2], in0=x16, scalar1=0x0808,
                    scalar2=4, op0=OP.bitwise_and,
                    op1=OP.logical_shift_left),
                    waits=mw({"pk%d" % slot: ev_sc}, ab_free))
                e2 = E("dve", lambda e: e.tensor_scalar(
                    out=ABU[:, slot, 2, 0:pw2], in0=x16, scalar1=0x0707,
                    scalar2=2, op0=OP.bitwise_and,
                    op1=OP.logical_shift_left),
                    waits=mw({"pk%d" % slot: ev_sc}, ab_free))
                e3 = E("dve", lambda e: e.tensor_tensor(
                    out=ABU[:, slot, 0, 0:pw2], in0=ABU[:, slot, 0, 0:pw2],
                    in1=ABU[:, slot, 2, 0:pw2],
                    op=OP.bitwise_or), waits={"dve": max(e1, e2)})
                ca0 = E("act", lambda e: e.activation(
                    out=WNAT[:, slot, 0, 0:pkw],
                    in_=ABU[:, slot, 0, 0:pw2].bitcast(FP8),
                    func=AF.Copy, scale=LUTS),
                    waits=mw({"dve": e3}, wnat_free))
                e4 = E("dve", lambda e: e.tensor_scalar(
                    out=ABU[:, slot, 1, 0:pw2], in0=x16, scalar1=0x7070,
                    scalar2=2, op0=OP.bitwise_and,
                    op1=OP.logical_shift_right),
                    waits=mw({"pk%d" % slot: ev_sc}, ab_free))
                e5 = E("dve", lambda e: e.tensor_scalar(
                    out=ABU[:, slot, 2, 0:pw2], in0=x16, scalar1=0x8080,
                    scalar2=None, op0=OP.bitwise_and),
                    waits={"dve": e3})
                e6 = E("dve", lambda e: e.tensor_tensor(
                    out=ABU[:, slot, 1, 0:pw2], in0=ABU[:, slot, 1, 0:pw2],
                    in1=ABU[:, slot, 2, 0:pw2],
                    op=OP.bitwise_or), waits={"dve": max(e4, e5)})
                ca1 = E("act", lambda e: e.activation(
                    out=WNAT[:, slot, 1, 0:pkw],
                    in_=ABU[:, slot, 1, 0:pw2].bitcast(FP8),
                    func=AF.Copy, scale=LUTS),
                    waits=mw({"dve": e6}, wnat_free))
                evxb = {}
                last_cs = 0
                for par, ca in ((0, ca0), (1, ca1)):
                    cs = E("dve", lambda e, p=par, s=slot: e.tensor_tensor(
                        out=WNAT[:, slot, p, 0:pkw].rearrange(
                            "p (b r) -> p b r", b=nsc, r=16),
                        in0=WNAT[:, slot, p, 0:pkw].rearrange(
                            "p (b r) -> p b r", b=nsc, r=16),
                        in1=SCB[:, s, 0:nsc, None].broadcast_to(
                            [128, nsc, 16]),
                        op=OP.mult),
                        waits={"act": ca, "pk%d" % slot: ev_sc})
                    last_cs = cs
                    wsem = "wg%d%d" % (slot, par)
                    evxb[wsem] = E("sp", lambda e, p=par:
                        e.dma_start_transpose(
                            out=(out_e if p == 0 else out_o),
                            in_=WNAT[:, slot, p, 0:pkw]),
                        waits=mw({"dve": cs}, xbar_free), dma=True,
                        sem=wsem)
                return {"xb": dict(evxb), "pk_next": {"dve": last_cs},
                        "ab_next": {"act": max(ca0, ca1),
                                    "dve": em.n["dve"]},
                        "wnat_next": dict(evxb)}

            p1b_done = {"tn": em.n["tn"], "dve": em.n["dve"],
                        "pe": em.n["pe"], "act": em.n["act"],
                        "ta": em.n["ta"]}
            ev_wgt_free = [dict(p1b_done) for _ in range(4)]
            ev_wdt_free = [dict(p1b_done), dict(p1b_done)]
            ev_inter_free = dict(p1b_done)
            xbdone = {"act": em.n["act"], "dve": em.n["dve"]}
            ev_pk_free = [dict(xbdone), dict(xbdone)]
            ev_ab_free = [dict(xbdone), dict(xbdone)]
            ev_wnat_free = [dict(xbdone), dict(xbdone)]
            ev_acc_free = {"th1": ev_h1t}
            ev_g1_free = [dict(p1b_done), dict(p1b_done)]
            ev_int_rd = 0

            for th in range(2):
                def deq_gateup(Bb):
                    dq = {0: None, 1: None}
                    if th == 1:
                        for mi in range(2):
                            ws = (2 * Bb + mi) % 4
                            sem = "sr%d" % ws
                            ev = E("pool", lambda e, B=Bb, m=mi, w=ws:
                                e.dma_start(out=WGT[:, w, :, :],
                                            in_=wspg[B, m]),
                                waits=ev_wgt_free[ws], dma=True, sem=sem)
                            dq[mi] = {sem: ev}
                        return dq
                    for mi, pkh, sch in ((0, gpk, gsc), (1, upk, usc)):
                        ws = (2 * Bb + mi) % 4
                        for i2 in range(2):
                            dv0 = emit_dequant(
                                pkh, sch, Bb * 256 + i2 * 128, 0, 0,
                                slot=i2,
                                out_e=WGT[:, ws, 0:16:2,
                                          i2 * 128:(i2 + 1) * 128],
                                out_o=WGT[:, ws, 1:16:2,
                                          i2 * 128:(i2 + 1) * 128],
                                pk_free=ev_pk_free[i2],
                                ab_free=ev_ab_free[i2],
                                wnat_free=ev_wnat_free[i2],
                                xbar_free=ev_wgt_free[ws])
                            ev_pk_free[i2] = dv0["pk_next"]
                            ev_ab_free[i2] = dv0["ab_next"]
                            ev_wnat_free[i2] = dv0["wnat_next"]
                            dq[mi] = mw(dq[mi], dv0["xb"])
                        evw = E("sp", lambda e, B=Bb, m=mi, w=ws:
                            e.dma_start(out=wspg[B, m],
                                        in_=WGT[:, w, :, :]),
                            waits=dq[mi], dma=True, sem="sw%d" % ws)
                        dq[mi] = mw(dq[mi], {"sw%d" % ws: evw})
                    return dq

                def deq_down(gidx, hp):
                    wd = hp % 2
                    if th == 1:
                        sem = "sdr%d" % wd
                        ev = E("pool", lambda e, g_=gidx, h_=hp, w=wd:
                            e.dma_start(
                                out=WDT[:, :, 4 * w:4 * w + 4, :],
                                in_=wspd[g_, h_]),
                            waits=ev_wdt_free[wd], dma=True, sem=sem)
                        return {sem: ev}
                    dsp = None
                    for i2 in range(2):
                        dv0 = emit_dequant(
                            dpk, dsc, (2 * hp + i2) * 128,
                            gidx * 512, gidx * 32, slot=i2,
                            out_e=WDT[:, 0, 4 * wd:4 * wd + 4,
                                      i2 * 128:(i2 + 1) * 128],
                            out_o=WDT[:, 1, 4 * wd:4 * wd + 4,
                                      i2 * 128:(i2 + 1) * 128],
                            pk_free=ev_pk_free[i2],
                            ab_free=ev_ab_free[i2],
                            wnat_free=ev_wnat_free[i2],
                            xbar_free=ev_wdt_free[wd],
                            pkw=512)
                        ev_pk_free[i2] = dv0["pk_next"]
                        ev_ab_free[i2] = dv0["ab_next"]
                        ev_wnat_free[i2] = dv0["wnat_next"]
                        dsp = mw(dsp, dv0["xb"])
                    evw = E("sp", lambda e, g_=gidx, h_=hp, w=wd:
                        e.dma_start(out=wspd[g_, h_],
                                    in_=WDT[:, :, 4 * w:4 * w + 4, :]),
                        waits=dsp, dma=True, sem="sdw%d" % wd)
                    return mw(dsp, {"sdw%d" % wd: evw})

                deqA = deq_gateup(0)
                last_evi = 0
                for Bb in range(NB):
                    gidx = Bb // 4
                    deq = deqA
                    wslot = {0: (2 * Bb) % 4, 1: (2 * Bb + 1) % 4}
                    dspA = None
                    if Bb % 4 == 3:
                        dspA = deq_down(gidx, 0)
                    if Bb + 1 < NB:
                        deqA = deq_gateup(Bb + 1)
                    ic0 = 2 * (Bb % 4)
                    ev_gmm = ev_umm = 0
                    for par in range(2):
                        for mi in range(2):
                            for ts in range(2):
                                bank = mi * 2 + ts
                                for kk in range(16):
                                    evm = E("pe", lambda e, k=kk,
                                            p=par, t=ts, b=bank, tt=th,
                                            w=wslot[mi]:
                                            e.matmul(
                                            out=pp[b][:],
                                            lhsT=WGT[:, w, k, p:256:2],
                                            rhs=H1T[:, k,
                                                tt * 1024 + t * 512:
                                                tt * 1024 + (t + 1) * 512],
                                            start=(k == 0), stop=(k == 15)),
                                        waits=mw(
                                            deq[mi], {"th1": ev_h1t},
                                            ev_psum_free[bank]))
                                if mi == 0:
                                    ev_gmm = evm
                                else:
                                    ev_umm = evm
                        for ts in range(2):
                            gb, ub = ts, 2 + ts
                            evg = E("act", lambda e, b=gb, t=ts:
                                e.activation(
                                    out=G1B[:, t, :], in_=pp[b][:],
                                    func=AF.Gelu_apprx_tanh),
                                waits=mw({"pe": ev_gmm}, ev_g1_free[ts]))
                            evi = E("dve", lambda e, b=ub, t=ts, p=par,
                                    i0=ic0: e.tensor_tensor(
                                    out=INTER[:, i0 + p,
                                              t * 512:(t + 1) * 512],
                                    in0=G1B[:, t, :], in1=pp[b][:],
                                    op=OP.mult),
                                waits=mw({"act": evg, "pe": ev_umm},
                                         ev_inter_free))
                            last_evi = evi
                            ev_g1_free[ts] = {"dve": evi}
                            ev_psum_free[gb] = {"act": evg}
                            ev_psum_free[ub] = {"dve": evi}
                    ev_wgt_free[wslot[0]] = mw({"pe": ev_gmm}, deq[0])
                    ev_wgt_free[wslot[1]] = mw({"pe": ev_umm}, deq[1])
                    if Bb % 4 == 3:
                        ev_int_dve = last_evi
                        for hp in range(8):
                            wd = hp % 2
                            dsp = dspA
                            if hp + 1 < 8:
                                dspA = deq_down(gidx, hp + 1)
                            for par2 in range(2):
                                for ts in range(2):
                                    db = 4 + ts
                                    for c8 in range(4):
                                        for par in range(2):
                                            evm = E("pe", lambda e, c=c8,
                                                    p=par, p2=par2, t=ts,
                                                    b=db, w=wd: e.matmul(
                                                    out=pp[b][:],
                                                    lhsT=WDT[:, p,
                                                             4 * w + c,
                                                             p2:256:2],
                                                    rhs=INTER[:, 2 * c + p,
                                                        t * 512:
                                                        (t + 1) * 512],
                                                    start=(c == 0 and
                                                           p == 0),
                                                    stop=(c == 3 and
                                                          p == 1)),
                                                waits=mw(
                                                    dsp,
                                                    {"dve": ev_int_dve},
                                                    ev_psum_free[db]))
                                    if gidx == 0:
                                        eva = E("dve", lambda e, b=db,
                                                h2=hp, p2=par2, t=ts:
                                            e.tensor_copy(
                                                out=ACC[:, 2 * h2 + p2,
                                                    t * 512:(t + 1) * 512],
                                                in_=pp[b][:]),
                                            waits=mw({"pe": evm},
                                                     ev_acc_free,
                                                     {"th1": ev_h1t},
                                                     ev_psum_free[db]))
                                    else:
                                        eva = E("dve", lambda e, b=db,
                                                h2=hp, p2=par2, t=ts:
                                            e.tensor_tensor(
                                                out=ACC[:, 2 * h2 + p2,
                                                    t * 512:(t + 1) * 512],
                                                in0=pp[b][:],
                                                in1=ACC[:, 2 * h2 + p2,
                                                    t * 512:(t + 1) * 512],
                                                op=OP.add),
                                            waits=mw({"pe": evm},
                                                     ev_psum_free[db]))
                                    ev_psum_free[db] = {"dve": eva}
                            ev_wdt_free[wd] = mw({"pe": em.n["pe"]}, dsp)
                        ev_inter_free = {"pe": em.n["pe"]}
                ev_dn = em.n["dve"]
                ev_pe_dn = em.n["pe"]
                ev_ffnr_free = {}
                ev_h1r_free = {}
                ev_z3_free = {"pe": ev_pe_dn}
                ev_outs_free = {}
                ev_accbf_free = {}
                for jj in range(8):
                    jt = th * 8 + jj
                    evc = E("dve", lambda e, j=jj: e.tensor_copy(
                        out=ACCBF[:, :, :],
                        in_=ACC[:, :, j * 128:(j + 1) * 128]),
                        waits=mw({"dve": ev_dn}, ev_accbf_free))
                    evf = E("sp", lambda e: e.dma_start_transpose(
                        out=FFNR.rearrange("p (c t) -> p c t", c=16, t=128),
                        in_=ACCBF[:, :, :]),
                        waits=mw({"dve": evc, "pe": ev_pe_dn},
                                 ev_ffnr_free), dma=True, sem="trc")
                    ev_accbf_free = {"trc": evf}
                    evc2 = E("dve", lambda e, t=jt: e.tensor_copy(
                        out=ACCBF[:, :, :],
                        in_=H1T[:, :, t * 128:(t + 1) * 128]),
                        waits=mw({"trc": evf}, ev_accbf_free))
                    evh = E("sp", lambda e: e.dma_start_transpose(
                        out=H1R.rearrange("p (c t) -> p c t", c=16, t=128),
                        in_=ACCBF[:, :, :]),
                        waits=mw({"dve": evc2, "pe": ev_pe_dn},
                                 ev_h1r_free), dma=True, sem="trc")
                    ev_accbf_free = {"trc": evh}
                    evz = E("dve", lambda e: e.tensor_tensor(
                        out=Z3, in0=H1R, in1=FFNR, op=OP.add),
                        waits=mw({"trc": evh}, ev_z3_free))
                    ev_ffnr_free = {"dve": evz}
                    ev1 = E("dve", lambda e: e.tensor_reduce(
                        out=stat(S_SUM), in_=Z3, axis=AX, op=OP.add),
                        waits={"dve": evz})
                    ev2 = E("dve", lambda e: e.tensor_scalar(
                        out=stat(S_NM), in0=stat(S_SUM), scalar1=-1.0 / D,
                        scalar2=None, op0=OP.mult), waits={"dve": ev1})
                    ev3 = E("act", lambda e: e.activation(
                        out=H1R, in_=Z3, func=AF.Square, bias=stat(S_NM),
                        accum_out=stat(S_SSQ)),
                        waits={"dve": ev2, "trc": evh})
                    ev_h1r_free = {"act": ev3, "dve": ev1}
                    ev4 = E("act", lambda e: e.activation(
                        out=stat(S_LNV), in_=stat(S_SSQ), func=AF.Ln,
                        bias=stat(S_EPS), scale=1.0 / D),
                        waits={"act": ev3})
                    ev5 = E("act", lambda e: e.activation(
                        out=stat(S_RSTD), in_=stat(S_LNV), func=AF.Exp,
                        scale=-0.5), waits={"act": ev4})
                    ev6 = E("dve", lambda e: e.tensor_tensor(
                        out=stat(S_NMRS), in0=stat(S_NM),
                        in1=stat(S_RSTD), op=OP.mult),
                        waits={"act": ev5, "dve": ev2})
                    outp = OUTS.rearrange("p (b s h) -> p b h s",
                                          b=8, s=128, h=2)
                    ev7 = E("act", lambda e, o=outp: e.activation(
                        out=o, in_=Z3, func=AF.Identity,
                        scale=stat(S_RSTD), bias=stat(S_NMRS)),
                        waits=mw({"dve": ev6}, ev_outs_free))
                    ev_z3_free = {"act": ev7}
                    ev_st = E("sp", lambda e, t=jt: e.dma_start(
                        out=y_hbm[t * 128:(t + 1) * 128, :], in_=OUTS),
                        waits={"act": ev7}, dma=True, sem="st")
                    ev_outs_free = {"st": ev_st}
                ev_acc_free = {"dve": em.n["dve"]}
                ln3r = {"st": em.n["st"], "act": em.n["act"],
                        "dve": em.n["dve"], "trc": em.n["trc"]}
                ev_inter_free = mw(ev_inter_free, ln3r)
                ev_wgt_free = [mw(w, ln3r) for w in ev_wgt_free]
                ev_wdt_free = [mw(w, ln3r) for w in ev_wdt_free]
                ev_pk_free = [mw(w, ln3r) for w in ev_pk_free]
                ev_ab_free = [mw(w, ln3r) for w in ev_ab_free]

        E("sp", lambda e: e.nop(),
          waits={s: em.n[s] for s in SEMS}, noinc=True, sem="st")

        CSEMS = ("pe", "act", "dve")
        waited = {s: set() for s in CSEMS}
        for eng in ENGS:
            for w2, _, _, _ in em.q[eng]:
                for d, v in w2:
                    if d in CSEMS:
                        waited[d].add(v)
        rank = {s: {} for s in CSEMS}
        for s in CSEMS:
            for i, v in enumerate(sorted(waited[s])):
                rank[s][v] = i + 1
        carries = {s: set(rank[s]) for s in CSEMS}

        def emit(eng):
            cum = {s: 0 for s in CSEMS}

            def run(engobj):
                for w2, builder, inc, sem in em.q[eng]:
                    for d, v in w2:
                        if d in CSEMS:
                            engobj.wait_ge(sems[d], rank[d][v])
                        else:
                            engobj.wait_ge(sems[d], v)
                    inst = builder(engobj)
                    if inst is None or inc == 0:
                        continue
                    if sem in CSEMS:
                        cum[sem] += inc
                        if cum[sem] in carries[sem]:
                            inst.then_inc(sems[sem], 1)
                    else:
                        inst.then_inc(sems[sem], inc)
            return run

        block.tensor(emit("pe"))
        block.scalar(emit("act"))
        block.vector(emit("dve"))
        block.sync(emit("sp"))
        block.gpsimd(emit("pool"))

    return nc


def _host_layout(inputs):
    f32 = np.float32

    def rT(name):
        return np.ascontiguousarray(
            np.asarray(inputs[name], f32).reshape(16, 128).T)

    com = {
        "wq": np.ascontiguousarray(np.asarray(inputs["L0_attn_Wq"], f32)),
        "wk": np.ascontiguousarray(np.asarray(inputs["L0_attn_Wk"], f32)),
        "wv": np.ascontiguousarray(np.asarray(inputs["L0_attn_Wv"], f32)),
        "wo": np.ascontiguousarray(np.asarray(inputs["L0_attn_Wo"], f32)),
        "gT": rT("L0_attn_ln_g"), "bT": rT("L0_attn_ln_b"),
        "bqT": rT("L0_attn_bq"), "bkT": rT("L0_attn_bk"),
        "bvT": rT("L0_attn_bv"), "boT": rT("L0_attn_bo"),
        "gpk": np.ascontiguousarray(
            np.asarray(inputs["L0_ffn_gate_W_packed"], np.int32)),
        "upk": np.ascontiguousarray(
            np.asarray(inputs["L0_ffn_up_W_packed"], np.int32)),
        "dpk": np.ascontiguousarray(
            np.asarray(inputs["L0_ffn_down_W_packed"], np.int32)),
        "gsc": np.ascontiguousarray(
            np.asarray(inputs["L0_ffn_gate_scales"], f32)),
        "usc": np.ascontiguousarray(
            np.asarray(inputs["L0_ffn_up_scales"], f32)),
        "dsc": np.ascontiguousarray(
            np.asarray(inputs["L0_ffn_down_scales"], f32)),
    }
    q = np.arange(128)[:, None]
    k = np.arange(128)[None, :]
    com["mask"] = np.where(k <= q, 0.0, NEGB).astype(f32)
    return com


def kernel(**inputs):
    X = np.ascontiguousarray(np.asarray(inputs["L0_attn_X"], np.float32))
    com = _host_layout(inputs)
    nc = build_nc()
    in_maps = [dict(com, x=np.ascontiguousarray(X[b])) for b in range(8)]
    res = run_bass_kernel_spmd(nc, in_maps, core_ids=list(range(8)))
    return np.stack([np.asarray(res.results[b]["y"]) for b in range(8)])


# version 12
# speedup vs baseline: 1.1573x; 1.1573x over previous
import contextlib
import numpy as np
import ml_dtypes
import concourse.bass as bass
import concourse.mybir as mybir
from concourse.bass_utils import run_bass_kernel_spmd

F32 = mybir.dt.float32
BF16 = mybir.dt.bfloat16
I32 = mybir.dt.int32
U8 = mybir.dt.uint8
U16 = mybir.dt.uint16
FP8 = mybir.dt.float8e4
OP = mybir.AluOpType
AF = mybir.ActivationFunctionType
AX = mybir.AxisListType.X

D = 2048
NH = 16
FN = 16384
NB = 64
EPS = 1e-5
ISQ = float(1.0 / np.sqrt(128.0))
NEGB = -1.0e9
LUTS = 64.0

STAGE = 3
DEBUG = False
SUB = 99

ENGS = ("pe", "act", "dve", "sp", "pool")
DMASEMS = ("x", "pan0", "pan1", "pk0", "pk1", "cst",
           "txn", "tv0", "tv1", "ta", "tn", "th1", "trc", "st",
           "wg00", "wg01", "wg10", "wg11",
           "sw0", "sw1", "sw2", "sw3", "sr0", "sr1", "sr2", "sr3",
           "sdw0", "sdw1", "sdr0", "sdr1")
SEMS = ENGS[:3] + DMASEMS
SZ = {F32: 4, BF16: 2, I32: 4, U8: 1, U16: 2, FP8: 1}


class Em:
    def __init__(self):
        self.q = {e: [] for e in ENGS}
        self.n = {s: 0 for s in SEMS}
        self.lastw = {e: {d: 0 for d in SEMS} for e in ENGS}

    def op(self, eng, builder, waits=None, dma=False, sem=None,
           noinc=False):
        if sem is None:
            sem = eng
            assert not dma or eng not in ("sp", "pool")
        w2 = []
        if waits:
            for d, v in sorted(waits.items()):
                if v is None or v <= 0:
                    continue
                if d == sem and not dma:
                    assert v <= self.n[sem], (sem, v, self.n[sem])
                if v > self.lastw[eng][d]:
                    w2.append((d, v))
                    self.lastw[eng][d] = v
        if noinc:
            self.q[eng].append((w2, builder, 0, sem))
            return None
        inc = 16 if dma else 1
        self.q[eng].append((w2, builder, inc, sem))
        self.n[sem] += inc
        return self.n[sem]


def mw(*dicts):
    out = {}
    for dd in dicts:
        if not dd:
            continue
        for k, v in dd.items():
            if v is None:
                continue
            out[k] = max(out.get(k, 0), v)
    return out


def build_nc(stage=None, debug=None, sub=None):
    if stage is None:
        stage = STAGE
    if debug is None:
        debug = DEBUG
    if sub is None:
        sub = SUB
    nc = bass.Bass("TRN2", target_bir_lowering=False)

    x_hbm = nc.dram_tensor("x", [2048, D], F32, kind="ExternalInput").ap()
    wq = nc.dram_tensor("wq", [D, D], F32, kind="ExternalInput").ap()
    wk = nc.dram_tensor("wk", [D, D], F32, kind="ExternalInput").ap()
    wv = nc.dram_tensor("wv", [D, D], F32, kind="ExternalInput").ap()
    wo = nc.dram_tensor("wo", [D, D], F32, kind="ExternalInput").ap()
    gT = nc.dram_tensor("gT", [128, 16], F32, kind="ExternalInput").ap()
    bT = nc.dram_tensor("bT", [128, 16], F32, kind="ExternalInput").ap()
    bqT = nc.dram_tensor("bqT", [128, 16], F32, kind="ExternalInput").ap()
    bkT = nc.dram_tensor("bkT", [128, 16], F32, kind="ExternalInput").ap()
    bvT = nc.dram_tensor("bvT", [128, 16], F32, kind="ExternalInput").ap()
    boT = nc.dram_tensor("boT", [128, 16], F32, kind="ExternalInput").ap()
    maskh = nc.dram_tensor("mask", [128, 128], F32, kind="ExternalInput").ap()
    gpk = nc.dram_tensor("gpk", [FN, 1024], I32, kind="ExternalInput").ap()
    upk = nc.dram_tensor("upk", [FN, 1024], I32, kind="ExternalInput").ap()
    dpk = nc.dram_tensor("dpk", [D, 8192], I32, kind="ExternalInput").ap()
    gsc = nc.dram_tensor("gsc", [FN, 64], F32, kind="ExternalInput").ap()
    usc = nc.dram_tensor("usc", [FN, 64], F32, kind="ExternalInput").ap()
    dsc = nc.dram_tensor("dsc", [D, 512], F32, kind="ExternalInput").ap()
    y_hbm = nc.dram_tensor("y", [2048, D], F32, kind="ExternalOutput").ap()
    wspg = nc.dram_tensor("wspg", [NB, 2, 128, 4096], BF16,
                          kind="Internal").ap()
    wspd = nc.dram_tensor("wspd", [16, 8, 128, 2048], BF16,
                          kind="Internal").ap()
    dbg = {}
    if debug:
        dbg["A"] = nc.dram_tensor("dbgA", [128, 32768], BF16,
                                  kind="ExternalOutput").ap()
        dbg["B"] = nc.dram_tensor("dbgB", [128, 32768], BF16,
                                  kind="ExternalOutput").ap()
    em = Em()
    E = em.op

    with contextlib.ExitStack() as stack:
        ec = stack.enter_context
        regA = ec(nc.sbuf_tensor("regA", [128, 65536], U8))
        regB = ec(nc.sbuf_tensor("regB", [128, 65536], U8))
        regC = ec(nc.sbuf_tensor("regC", [128, 55296], U8))
        regD = ec(nc.sbuf_tensor("regD", [128, 26112], U8))
        pp = [ec(nc.psum_tensor(f"pp{i}", [128, 512], F32)) for i in range(8)]
        sems = {s: ec(nc.semaphore("sem_" + s)) for s in SEMS}
        block = ec(nc.Block())

        def view(reg, off, dt_, n):
            return reg.ap()[:, off:off + n * SZ[dt_]].bitcast(dt_)

        KT = view(regA, 0, BF16, 32768).rearrange(
            "p (s c t) -> p s c t", s=4, c=16, t=512)
        H1P = view(regA, 0, BF16, 32768).rearrange(
            "p (s j d) -> p s j d", s=4, j=4, d=2048)
        ACC = view(regA, 0, F32, 16384).rearrange(
            "p (c t) -> p c t", c=16, t=1024)
        VN = view(regB, 0, BF16, 32768).rearrange(
            "p (c d) -> p c d", c=16, d=2048)
        H1T = view(regB, 0, BF16, 32768).rearrange(
            "p (c t) -> p c t", c=16, t=2048)
        XNH = view(regC, 0, BF16, 16384).rearrange(
            "p (c t) -> p c t", c=16, t=1024)
        XNS = view(regC, 0, BF16, 8192).rearrange(
            "p (c t) -> p c t", c=16, t=512)
        ATTNT = view(regC, 0, BF16, 8192).rearrange(
            "p (c t) -> p c t", c=16, t=512)
        QTB = view(regC, 16384, BF16, 1024).rearrange(
            "p (a t) -> p a t", a=2, t=512)
        SROW = view(regC, 18432, F32, 2048)
        AROW = view(regC, 26624, BF16, 2048)
        ATB = view(regC, 30720, BF16, 4096).rearrange(
            "p (a c t) -> p a c t", a=2, c=16, t=128)
        AVT = view(regC, 38912, BF16, 8192).rearrange(
            "p (c t) -> p c t", c=16, t=512)
        ATTNN = view(regC, 38912, BF16, 8192).rearrange(
            "p (j d) -> p j d", j=4, d=2048)
        INTER = view(regC, 0, BF16, 8192).rearrange(
            "p (c t) -> p c t", c=8, t=1024)
        WGT = view(regC, 16384, BF16, 16384).rearrange(
            "p (a c n) -> p a c n", a=4, c=16, n=256)
        Z3 = view(regC, 0, F32, 2048)
        OUTS = view(regC, 8192, F32, 2048)
        H1R = view(regC, 16384, BF16, 2048)
        FFNR = view(regC, 20480, BF16, 2048)
        G1B = view(regC, 49152, BF16, 1024).rearrange(
            "p (a t) -> p a t", a=2, t=512)
        SCB = view(regC, 51200, F32, 128).rearrange(
            "p (a c) -> p a c", a=2, c=64)
        XB = view(regD, 0, F32, 2048)
        XNB = view(regD, 8192, BF16, 2048)
        PANEL = view(regD, 12288, BF16, 4096).rearrange(
            "p (a c n) -> p a c n", a=2, c=16, n=128)
        VTS = view(regD, 20480, BF16, 2048).rearrange(
            "p (a t) -> p a t", a=2, t=1024)
        PKU = view(regD, 0, U8, 2048).rearrange(
            "p (a k) -> p a k", a=2, k=1024)
        ABU = view(regD, 2048, U16, 3072).rearrange(
            "p (s a k) -> p s a k", s=2, a=3, k=512)
        WNAT = view(regD, 8192, BF16, 4096).rearrange(
            "p (s a k) -> p s a k", s=2, a=2, k=1024)
        WDT = view(regD, 16384, BF16, 4096).rearrange(
            "p (a c h) -> p a c h", a=2, c=8, h=256)
        ACCBF = view(regD, 0, BF16, 2048).rearrange(
            "p (c t) -> p c t", c=16, t=128)
        AROW2 = view(regD, 20480, BF16, 2048)
        C0 = 24576
        GTT = view(regD, C0, F32, 16)
        BTT = view(regD, C0 + 64, F32, 16)
        BQT = view(regD, C0 + 128, F32, 16)
        BKT = view(regD, C0 + 192, F32, 16)
        BVT = view(regD, C0 + 256, F32, 16)
        BOT = view(regD, C0 + 320, F32, 16)
        MSK = view(regD, C0 + 384, F32, 128)
        ST0 = C0 + 1024

        def stat(i):
            return view(regD, ST0 + 4 * i, F32, 1)

        (S_SUM, S_NM, S_LNV, S_RSTD, S_NMRS, S_SSQ, S_NMX, S_RS, S_RCP,
         S_EPS, S_NMX1, S_RS1, S_RCP1, S_CM0, S_CM1, S_CM2, S_CM3) = \
            range(17)
        ev_eps = E("dve", lambda e: e.memset(stat(S_EPS), EPS))

        ev_consts = 0
        for apd, src in ((GTT, gT), (BTT, bT), (BQT, bqT), (BKT, bkT),
                         (BVT, bvT), (BOT, boT), (MSK, maskh)):
            ev_consts = E("pool", lambda e, a=apd, s=src:
                          e.dma_start(out=a, in_=s), dma=True, sem="cst")

        def emit_ln(ev_x_pool, pre_dve, junkap, outap, junk_waits=None,
                    out_waits=None):
            ev1 = E("dve", lambda e: e.tensor_reduce(
                out=stat(S_SUM), in_=XB, axis=AX, op=OP.add),
                waits=mw({"x": ev_x_pool}, pre_dve))
            ev2 = E("dve", lambda e: e.tensor_scalar(
                out=stat(S_NM), in0=stat(S_SUM), scalar1=-1.0 / D,
                scalar2=None, op0=OP.mult), waits={"dve": ev1})
            ev3 = E("act", lambda e, j=junkap: e.activation(
                out=j, in_=XB, func=AF.Square, bias=stat(S_NM),
                accum_out=stat(S_SSQ)),
                waits=mw({"dve": ev2, "x": ev_x_pool}, junk_waits))
            ev4 = E("act", lambda e: e.activation(
                out=stat(S_LNV), in_=stat(S_SSQ), func=AF.Ln,
                bias=stat(S_EPS), scale=1.0 / D),
                waits={"act": ev3, "dve": ev_eps})
            ev5 = E("act", lambda e: e.activation(
                out=stat(S_RSTD), in_=stat(S_LNV), func=AF.Exp,
                scale=-0.5), waits={"act": ev4})
            ev6 = E("dve", lambda e: e.tensor_tensor(
                out=stat(S_NMRS), in0=stat(S_NM), in1=stat(S_RSTD),
                op=OP.mult), waits={"act": ev5, "dve": ev2})
            ev7 = E("act", lambda e, o=outap: e.activation(
                out=o, in_=XB, func=AF.Identity, scale=stat(S_RSTD),
                bias=stat(S_NMRS)), waits=mw({"dve": ev6}, out_waits))
            return ev7, ev1

        ev_xb_free = {}
        ev_panel_free = [{}, {}]
        ev_vts_free = [{}, {}]
        ev_psum_free = [{} for _ in range(8)]
        ev_xnt_free = {}
        ev_xnb_free = {}
        ev_h1t = 0

        for th in range(2):
            ev_gb = 0
            for j8 in range(8):
                jt = th * 8 + j8
                ev_x = E("pool", lambda e, j=jt: e.dma_start(
                    out=XB, in_=x_hbm[j * 128:(j + 1) * 128, :]),
                    waits=ev_xb_free, dma=True, sem="x")
                ev_ln, ev_lnr = emit_ln(ev_x, None, XNB, XNB,
                                        junk_waits=ev_xnb_free)
                ev_xb_free = {"act": ev_ln, "dve": ev_lnr}
                ev_tp = E("sp", lambda e, j=j8: e.dma_start_transpose(
                    out=XNH[:, :, j * 128:(j + 1) * 128], in_=XNB),
                    waits=mw({"act": ev_ln}, ev_xnt_free), dma=True,
                    sem="txn")
                ev_xnb_free = {"txn": ev_tp}
                for dc in range(16):
                    ev_gb = E("dve", lambda e, c=dc, j=j8: e.tensor_scalar(
                        out=XNH[:, c, j * 128:(j + 1) * 128],
                        in0=XNH[:, c, j * 128:(j + 1) * 128],
                        scalar1=GTT[:, c:c + 1], scalar2=BTT[:, c:c + 1],
                        op0=OP.mult, op1=OP.add),
                        waits={"txn": ev_tp, "cst": ev_consts})
            for wi, whbm, bap in ((0, wv, BVT), (1, wk, BKT)):
                for c in range(16):
                    slot = c % 2
                    ev_pl = E("pool", lambda e, s=slot, cc=c, W=whbm:
                        e.dma_start(
                            out=PANEL[:, s, :, :],
                            in_=W.rearrange("(ci p) n -> p ci n", p=128)
                                [:, :, cc * 128:(cc + 1) * 128]),
                        waits=ev_panel_free[slot], dma=True,
                        sem="pan%d" % slot)
                    eve = None
                    for tg in range(2):
                        bank = (c * 2 + tg) % 4
                        for dc in range(16):
                            evm = E("pe", lambda e, s=slot, d=dc, t=tg,
                                    b=bank: e.matmul(
                                    out=pp[b][:], lhsT=PANEL[:, s, d, :],
                                    rhs=XNH[:, d, t * 512:(t + 1) * 512],
                                    start=(d == 0), stop=(d == 15)),
                                waits=mw({"pan%d" % slot: ev_pl,
                                          "dve": ev_gb},
                                         ev_psum_free[bank]))
                        ev_panel_free[slot] = {"pe": evm}
                        ev_xnt_free = {"pe": evm}
                        if wi == 1:
                            sup = th * 2 + tg
                            eve = E("act", lambda e, b=bank, s=sup, cc=c,
                                    ba=bap: e.activation(
                                    out=KT[:, s, cc, :], in_=pp[b][:],
                                    func=AF.Identity,
                                    bias=ba[:, cc:cc + 1]),
                                waits={"pe": evm, "cst": ev_consts})
                        else:
                            eve = E("act", lambda e, b=bank, t=tg, cc=c,
                                    s=slot, ba=bap: e.activation(
                                    out=VTS[:, s, t * 512:(t + 1) * 512],
                                    in_=pp[b][:], func=AF.Identity,
                                    bias=ba[:, cc:cc + 1]),
                                waits=mw({"pe": evm, "cst": ev_consts},
                                         ev_vts_free[slot]))
                        ev_psum_free[bank] = {"act": eve}
                    if wi == 0:
                        evv = E("sp", lambda e, s=c % 2, cc=c, t=th:
                            e.dma_start_transpose(
                                out=VN[:, t * 8:(t + 1) * 8,
                                       cc * 128:(cc + 1) * 128],
                                in_=VTS[:, s, :]),
                            waits={"act": eve}, dma=True,
                            sem="tv%d" % (c % 2))
                        ev_vts_free[c % 2] = {"tv%d" % (c % 2): evv}

        if debug and stage == 1:
            E("sp", lambda e: e.dma_start(
                out=dbg["A"], in_=view(regA, 0, BF16, 32768)),
                waits={"st": em.n["st"], "act": em.n["act"], "dve": em.n["dve"],
                       "pe": em.n["pe"]}, dma=True, sem="st")
            E("sp", lambda e: e.dma_start(
                out=dbg["B"], in_=view(regB, 0, BF16, 32768)),
                waits={"st": em.n["st"], "tv0": em.n["tv0"], "tv1": em.n["tv1"]},
                dma=True, sem="st")
        dbgw = {"st": em.n["st"]} if debug else {}

        ev_tv0 = em.n["tv0"]
        ev_tv1 = em.n["tv1"]
        ev_h1_act = 0
        if stage >= 2:
            ev_qt_free = [{}, {}]
            ev_qpanel_free = [dict(ev_panel_free[0]), dict(ev_panel_free[1])]
            ev_xns_free = dict(ev_xnt_free)
            ev_avt_free = {}
            ev_srow_free = [{}, {}]
            ev_arow_free = [{}, {"tv0": ev_tv0, "tv1": ev_tv1}]
            ev_atb_free = [{}, {}]

            for g in (3, 2, 1, 0):
                for j in range(4):
                    jt = 4 * g + j
                    ev_x = E("pool", lambda e, t=jt: e.dma_start(
                        out=XB, in_=x_hbm[t * 128:(t + 1) * 128, :]),
                        waits=ev_xb_free, dma=True, sem="x")
                    ev_ln, ev_lnr = emit_ln(ev_x, None, XNB, XNB,
                                            junk_waits=ev_xnb_free)
                    ev_xb_free = {"act": ev_ln, "dve": ev_lnr}
                    ev_tp = E("sp", lambda e, j_=j: e.dma_start_transpose(
                        out=XNS[:, :, j_ * 128:(j_ + 1) * 128], in_=XNB),
                        waits=mw({"act": ev_ln}, ev_xns_free), dma=True,
                        sem="txn")
                    ev_xnb_free = {"txn": ev_tp}
                    for dc in range(16):
                        ev_gb = E("dve", lambda e, c=dc, j_=j:
                            e.tensor_scalar(
                                out=XNS[:, c, j_ * 128:(j_ + 1) * 128],
                                in0=XNS[:, c, j_ * 128:(j_ + 1) * 128],
                                scalar1=GTT[:, c:c + 1],
                                scalar2=BTT[:, c:c + 1],
                                op0=OP.mult, op1=OP.add),
                            waits={"txn": ev_tp, "cst": ev_consts})
                nk = 4 * (g + 1)
                ev_last_S = 0
                ev_exps = [0, 0]
                prev_tail = None
                ev_srow_free[1] = mw(ev_srow_free[1], ev_xb_free)
                for h in range(16) if sub >= 2 else []:
                    slot = h % 2
                    ev_pl = E("pool", lambda e, s=slot, hh=h: e.dma_start(
                        out=PANEL[:, s, :, :],
                        in_=wq.rearrange("(ci p) n -> p ci n", p=128)
                            [:, :, hh * 128:(hh + 1) * 128]),
                        waits=ev_qpanel_free[slot], dma=True,
                        sem="pan%d" % slot)
                    qb = h % 4
                    for dc in range(16):
                        evm = E("pe", lambda e, s=slot, d=dc, b=qb:
                            e.matmul(
                                out=pp[b][:], lhsT=PANEL[:, s, d, :],
                                rhs=XNS[:, d, :],
                                start=(d == 0), stop=(d == 15)),
                            waits=mw({"pan%d" % slot: ev_pl,
                                      "dve": ev_gb},
                                     ev_psum_free[qb]))
                    ev_qpanel_free[slot] = {"pe": evm}
                    ev_qe = E("act", lambda e, b=qb, s=slot, hh=h:
                        e.activation(out=QTB[:, s, :], in_=pp[b][:],
                                     func=AF.Identity,
                                     bias=BQT[:, hh:hh + 1]),
                        waits=mw({"pe": evm, "cst": ev_consts},
                                 ev_qt_free[slot]))
                    ev_psum_free[qb] = {"act": ev_qe}
                    def emit_tail(pv):
                        qt_, nkq_, klen_, ar_, hh_ = pv
                        par_ = qt_ % 2
                        ev_rcp = E("dve", lambda e, pa=par_: e.reciprocal(
                            out=stat(S_RCP + 4 * pa), in_=stat(S_RS + 4 * pa)),
                            waits={"act": ev_exps[par_]})
                        ev_an = E("dve", lambda e, kl=klen_, ar=ar_, pa=par_:
                            e.tensor_scalar(
                                out=ar[:, 0:kl], in0=ar[:, 0:kl],
                                scalar1=stat(S_RCP + 4 * pa), scalar2=None,
                                op0=OP.mult),
                            waits={"dve": ev_rcp})
                        ev_ta = E("sp", lambda e, kl=klen_, nq=nkq_,
                                  a=par_, ar=ar_:
                            e.dma_start_transpose(
                                out=ATB[:, a, 0:nq, :],
                                in_=ar[:, 0:kl]),
                            waits=mw({"dve": ev_an}, ev_atb_free[par_]),
                            dma=True, sem="ta")
                        ev_arow_free[par_] = {"ta": ev_ta}
                        ab = 6 + par_
                        evav = 0
                        for kc in range(nkq_):
                            evav = E("pe", lambda e, k=kc, hh=hh_, b=ab,
                                     a=par_, nq=nkq_: e.matmul(
                                    out=pp[b][:, 0:128],
                                    lhsT=VN[:, k, hh * 128:(hh + 1) * 128],
                                    rhs=ATB[:, a, k, :],
                                    start=(k == 0), stop=(k == nq - 1)),
                                waits=mw({"ta": ev_ta, "tv0": ev_tv0,
                                          "tv1": ev_tv1},
                                         ev_psum_free[ab]))
                        ev_atb_free[par_] = {"pe": evav}
                        eva2 = E("dve", lambda e, b=ab, hh=hh_, q=qt_:
                            e.tensor_copy(
                                out=AVT[:, hh, q * 128:(q + 1) * 128],
                                in_=pp[b][:, 0:128]),
                            waits=mw({"pe": evav}, ev_avt_free))
                        ev_psum_free[ab] = {"dve": eva2}
                        return eva2

                    for qt in range(4):
                        nkq = g * 4 + qt + 1
                        klen = nkq * 128
                        par = qt % 2
                        srow = SROW if par == 0 else XB
                        ev_cm = 0
                        for ks in range(g + 1):
                            w = 512 if ks < g else (qt + 1) * 128
                            sb = 4 + (ks % 2)
                            evs = E("pe", lambda e, b=sb, s=slot, q=qt,
                                    k=ks, ww=w, hh=h: e.matmul(
                                    out=pp[b][:, 0:ww],
                                    lhsT=QTB[:, s, q * 128:(q + 1) * 128],
                                    rhs=KT[:, k, hh, 0:ww],
                                    start=True, stop=True),
                                waits=mw({"act": ev_qe},
                                         ev_psum_free[sb]))
                            ev_last_S = evs
                            evd = 0
                            if ks < g:
                                eve = E("act", lambda e, b=sb, k=ks,
                                        ww=w, sr=srow: e.activation(
                                        out=sr[:, k * 512:k * 512 + ww],
                                        in_=pp[b][:, 0:ww],
                                        func=AF.Identity, scale=ISQ),
                                    waits=mw({"pe": evs}, ev_srow_free[par]))
                            else:
                                if qt > 0:
                                    E("act", lambda e, b=sb, q=qt,
                                      gg=g, sr=srow: e.activation(
                                        out=sr[:, gg * 512:
                                                 gg * 512 + q * 128],
                                        in_=pp[b][:, 0:q * 128],
                                        func=AF.Identity, scale=ISQ),
                                      waits=mw({"pe": evs},
                                               ev_srow_free[par]))
                                eve = E("act", lambda e, b=sb, q=qt,
                                        gg=g, sr=srow: e.activation(
                                        out=sr[:, gg * 512 + q * 128:
                                                 gg * 512 + (q + 1) * 128],
                                        in_=pp[b][:,
                                                  q * 128:(q + 1) * 128],
                                        func=AF.Identity, scale=ISQ),
                                    waits=mw({"pe": evs}, ev_srow_free[par]))
                                evd = E("dve", lambda e, q=qt, gg=g, sr=srow:
                                    e.tensor_tensor(
                                        out=sr[:, gg * 512 + q * 128:
                                                 gg * 512 + (q + 1) * 128],
                                        in0=sr[:, gg * 512 + q * 128:
                                                 gg * 512 + (q + 1) * 128],
                                        in1=MSK, op=OP.add),
                                    waits={"act": eve, "cst": ev_consts})
                            ev_psum_free[sb] = {"act": eve}
                            cw = w if ks == g else 512
                            ev_cm = E("dve", lambda e, k=ks, ww=cw, gg=g,
                                      q=qt, sr=srow: e.tensor_reduce(
                                    out=stat(S_CM0 + k),
                                    in_=sr[:, k * 512:k * 512 + ww]
                                    if k < gg else
                                    sr[:, gg * 512:gg * 512 +
                                         (q + 1) * 128],
                                    axis=AX, op=OP.max),
                                waits={"act": eve, "dve": evd})
                        evmx = E("dve", lambda e, n_=g + 1, pa=par:
                            e.tensor_reduce(
                            out=stat(S_NMX + 4 * pa),
                            in_=view(regD, ST0 + 4 * S_CM0, F32, n_),
                            axis=AX, op=OP.max, negate=True),
                            waits={"dve": ev_cm})
                        arow = AROW if par == 0 else AROW2
                        ev_exps[par] = E("act", lambda e, kl=klen, ar=arow,
                                         pa=par, sr=srow: e.activation(
                            out=ar[:, 0:kl], in_=sr[:, 0:kl],
                            func=AF.Exp, bias=stat(S_NMX + 4 * pa),
                            accum_out=stat(S_RS + 4 * pa)),
                            waits=mw({"dve": evmx}, ev_arow_free[par]))
                        ev_srow_free[par] = {"act": ev_exps[par],
                                             "dve": evmx}
                        if prev_tail is not None:
                            ev_ave = emit_tail(prev_tail)
                        prev_tail = (qt, nkq, klen, arow, h)
                    ev_qt_free[slot] = {"pe": ev_last_S}
                ev_ave = emit_tail(prev_tail)
                prev_tail = None
                ev_avt_all = ev_ave
                ev_xb_free = mw(ev_xb_free, ev_srow_free[1])
                ev_srow_free[1] = {}

                dbg_st = {}
                ev_avt_free = {}
                ev_oe = None
                for c in range(16):
                    slot = c % 2
                    ev_pl = E("pool", lambda e, s=slot, cc=c: e.dma_start(
                        out=PANEL[:, s, :, :],
                        in_=wo.rearrange("(ci p) n -> p ci n", p=128)
                            [:, :, cc * 128:(cc + 1) * 128]),
                        waits=ev_qpanel_free[slot], dma=True,
                        sem="pan%d" % slot)
                    ob = c % 4
                    for dc in range(16):
                        evm = E("pe", lambda e, s=slot, d=dc, b=ob:
                            e.matmul(
                                out=pp[b][:], lhsT=PANEL[:, s, d, :],
                                rhs=AVT[:, d, :],
                                start=(d == 0), stop=(d == 15)),
                            waits=mw({"pan%d" % slot: ev_pl,
                                      "dve": ev_avt_all},
                                     ev_psum_free[ob]))
                    ev_qpanel_free[slot] = {"pe": evm}
                    ev_oe = E("act", lambda e, b=ob, cc=c: e.activation(
                        out=ATTNT[:, cc, :], in_=pp[b][:],
                        func=AF.Identity, bias=BOT[:, cc:cc + 1]),
                        waits=mw({"pe": evm, "cst": ev_consts}, dbg_st))
                    ev_psum_free[ob] = {"act": ev_oe}
                ev_o_pe = em.n["pe"]
                if sub < 8:
                    continue
                ev_attnN = 0
                for c in range(16):
                    ev_attnN = E("sp", lambda e, cc=c:
                        e.dma_start_transpose(
                            out=ATTNN[:, :, cc * 128:(cc + 1) * 128],
                            in_=ATTNT[:, cc, :]),
                        waits=mw({"act": ev_oe, "pe": ev_o_pe}, dbg_st),
                        dma=True, sem="tn")
                for j in range(4):
                    jt = 4 * g + j
                    ev_x = E("pool", lambda e, t=jt: e.dma_start(
                        out=XB, in_=x_hbm[t * 128:(t + 1) * 128, :]),
                        waits=ev_xb_free, dma=True, sem="x")
                    ev_z = E("dve", lambda e, j_=j: e.tensor_tensor(
                        out=XB, in0=XB, in1=ATTNN[:, j_, :], op=OP.add),
                        waits={"x": ev_x, "tn": ev_attnN})
                    h1ap = H1P[:, g, j, :].rearrange(
                        "p (b h s) -> p b s h", b=8, h=2, s=128)
                    ev_ln, ev_lnr = emit_ln(
                        ev_x, {"dve": ev_z}, XNB, h1ap,
                        junk_waits=ev_xnb_free,
                        out_waits=mw({"pe": ev_last_S}, dbgw))
                    ev_xb_free = {"act": ev_ln, "dve": ev_lnr}
                    ev_xnb_free = {"act": ev_ln}
                    ev_h1_act = ev_ln
                ev_xns_free = mw({"tn": em.n["tn"], "pe": em.n["pe"]},
                                 {"st": em.n["st"]} if debug else {})
                ev_avt_free = mw({"dve": em.n["dve"], "tn": em.n["tn"]},
                                 {"st": em.n["st"]} if debug else {})

        if debug and stage == 2:
            E("sp", lambda e: e.dma_start(
                out=dbg["A"], in_=view(regA, 0, BF16, 32768)),
                waits={"st": em.n["st"], "act": em.n["act"],
                       "dve": em.n["dve"], "pe": em.n["pe"]},
                dma=True, sem="st")
            E("sp", lambda e: e.dma_start(
                out=dbg["B"], in_=view(regB, 0, BF16, 32768)),
                waits={"st": em.n["st"]}, dma=True, sem="st")

        if stage >= 3:
            for jt in range(16):
                g, j = jt // 4, jt % 4
                ev_h1t = E("sp", lambda e, g_=g, j_=j, t=jt:
                    e.dma_start_transpose(
                        out=H1T[:, :, t * 128:(t + 1) * 128],
                        in_=H1P[:, g_, j_, :]),
                    waits={"act": ev_h1_act, "pe": em.n["pe"],
                           "dve": em.n["dve"]}, dma=True, sem="th1")

            def emit_dequant(pk_hbm, sc_hbm, r0, p0, s0, slot, out_e, out_o,
                             pk_free, ab_free, wnat_free, xbar_free,
                             pkw=1024):
                nsc = pkw // 16
                ev_pk = E("pool", lambda e: e.dma_start(
                    out=PKU[:, slot, 0:pkw],
                    in_=pk_hbm[r0:r0 + 128, p0:p0 + pkw]),
                    waits=pk_free, dma=True, sem="pk%d" % slot)
                ev_sc = E("pool", lambda e: e.dma_start(
                    out=SCB[:, slot, 0:nsc],
                    in_=sc_hbm[r0:r0 + 128, s0:s0 + nsc]),
                    waits=pk_free, dma=True, sem="pk%d" % slot)
                x16 = PKU[:, slot, 0:pkw].bitcast(U16)
                pw2 = pkw // 2
                e1 = E("dve", lambda e: e.tensor_scalar(
                    out=ABU[:, slot, 0, 0:pw2], in0=x16, scalar1=0x0808,
                    scalar2=4, op0=OP.bitwise_and,
                    op1=OP.logical_shift_left),
                    waits=mw({"pk%d" % slot: ev_sc}, ab_free))
                e2 = E("dve", lambda e: e.tensor_scalar(
                    out=ABU[:, slot, 2, 0:pw2], in0=x16, scalar1=0x0707,
                    scalar2=2, op0=OP.bitwise_and,
                    op1=OP.logical_shift_left),
                    waits=mw({"pk%d" % slot: ev_sc}, ab_free))
                e3 = E("dve", lambda e: e.tensor_tensor(
                    out=ABU[:, slot, 0, 0:pw2], in0=ABU[:, slot, 0, 0:pw2],
                    in1=ABU[:, slot, 2, 0:pw2],
                    op=OP.bitwise_or), waits={"dve": max(e1, e2)})
                ca0 = E("act", lambda e: e.activation(
                    out=WNAT[:, slot, 0, 0:pkw],
                    in_=ABU[:, slot, 0, 0:pw2].bitcast(FP8),
                    func=AF.Copy, scale=LUTS),
                    waits=mw({"dve": e3}, wnat_free))
                e4 = E("dve", lambda e: e.tensor_scalar(
                    out=ABU[:, slot, 1, 0:pw2], in0=x16, scalar1=0x7070,
                    scalar2=2, op0=OP.bitwise_and,
                    op1=OP.logical_shift_right),
                    waits=mw({"pk%d" % slot: ev_sc}, ab_free))
                e5 = E("dve", lambda e: e.tensor_scalar(
                    out=ABU[:, slot, 2, 0:pw2], in0=x16, scalar1=0x8080,
                    scalar2=None, op0=OP.bitwise_and),
                    waits={"dve": e3})
                e6 = E("dve", lambda e: e.tensor_tensor(
                    out=ABU[:, slot, 1, 0:pw2], in0=ABU[:, slot, 1, 0:pw2],
                    in1=ABU[:, slot, 2, 0:pw2],
                    op=OP.bitwise_or), waits={"dve": max(e4, e5)})
                ca1 = E("act", lambda e: e.activation(
                    out=WNAT[:, slot, 1, 0:pkw],
                    in_=ABU[:, slot, 1, 0:pw2].bitcast(FP8),
                    func=AF.Copy, scale=LUTS),
                    waits=mw({"dve": e6}, wnat_free))
                evxb = {}
                last_cs = 0
                for par, ca in ((0, ca0), (1, ca1)):
                    cs = E("dve", lambda e, p=par, s=slot: e.tensor_tensor(
                        out=WNAT[:, slot, p, 0:pkw].rearrange(
                            "p (b r) -> p b r", b=nsc, r=16),
                        in0=WNAT[:, slot, p, 0:pkw].rearrange(
                            "p (b r) -> p b r", b=nsc, r=16),
                        in1=SCB[:, s, 0:nsc, None].broadcast_to(
                            [128, nsc, 16]),
                        op=OP.mult),
                        waits={"act": ca, "pk%d" % slot: ev_sc})
                    last_cs = cs
                    wsem = "wg%d%d" % (slot, par)
                    evxb[wsem] = E("sp", lambda e, p=par:
                        e.dma_start_transpose(
                            out=(out_e if p == 0 else out_o),
                            in_=WNAT[:, slot, p, 0:pkw]),
                        waits=mw({"dve": cs}, xbar_free), dma=True,
                        sem=wsem)
                return {"xb": dict(evxb), "pk_next": {"dve": last_cs},
                        "ab_next": {"act": max(ca0, ca1),
                                    "dve": em.n["dve"]},
                        "wnat_next": dict(evxb)}

            p1b_done = {"tn": em.n["tn"], "dve": em.n["dve"],
                        "pe": em.n["pe"], "act": em.n["act"],
                        "ta": em.n["ta"]}
            ev_wgt_free = [dict(p1b_done) for _ in range(4)]
            ev_wdt_free = [dict(p1b_done), dict(p1b_done)]
            ev_inter_free = dict(p1b_done)
            xbdone = {"act": em.n["act"], "dve": em.n["dve"]}
            ev_pk_free = [dict(xbdone), dict(xbdone)]
            ev_ab_free = [dict(xbdone), dict(xbdone)]
            ev_wnat_free = [dict(xbdone), dict(xbdone)]
            ev_acc_free = {"th1": ev_h1t}
            ev_g1_free = [dict(p1b_done), dict(p1b_done)]
            ev_int_rd = 0

            for th in range(2):
                def deq_gateup(Bb):
                    dq = {0: None, 1: None}
                    if th == 1:
                        for mi in range(2):
                            ws = (2 * Bb + mi) % 4
                            sem = "sr%d" % ws
                            ev = E("pool", lambda e, B=Bb, m=mi, w=ws:
                                e.dma_start(out=WGT[:, w, :, :],
                                            in_=wspg[B, m]),
                                waits=ev_wgt_free[ws], dma=True, sem=sem)
                            dq[mi] = {sem: ev}
                        return dq
                    for mi, pkh, sch in ((0, gpk, gsc), (1, upk, usc)):
                        ws = (2 * Bb + mi) % 4
                        for i2 in range(2):
                            dv0 = emit_dequant(
                                pkh, sch, Bb * 256 + i2 * 128, 0, 0,
                                slot=i2,
                                out_e=WGT[:, ws, 0:16:2,
                                          i2 * 128:(i2 + 1) * 128],
                                out_o=WGT[:, ws, 1:16:2,
                                          i2 * 128:(i2 + 1) * 128],
                                pk_free=ev_pk_free[i2],
                                ab_free=ev_ab_free[i2],
                                wnat_free=ev_wnat_free[i2],
                                xbar_free=ev_wgt_free[ws])
                            ev_pk_free[i2] = dv0["pk_next"]
                            ev_ab_free[i2] = dv0["ab_next"]
                            ev_wnat_free[i2] = dv0["wnat_next"]
                            dq[mi] = mw(dq[mi], dv0["xb"])
                        evw = E("sp", lambda e, B=Bb, m=mi, w=ws:
                            e.dma_start(out=wspg[B, m],
                                        in_=WGT[:, w, :, :]),
                            waits=dq[mi], dma=True, sem="sw%d" % ws)
                        dq[mi] = mw(dq[mi], {"sw%d" % ws: evw})
                    return dq

                def deq_down(gidx, hp):
                    wd = hp % 2
                    if th == 1:
                        sem = "sdr%d" % wd
                        ev = E("pool", lambda e, g_=gidx, h_=hp, w=wd:
                            e.dma_start(
                                out=WDT[:, :, 4 * w:4 * w + 4, :],
                                in_=wspd[g_, h_]),
                            waits=ev_wdt_free[wd], dma=True, sem=sem)
                        return {sem: ev}
                    dsp = None
                    for i2 in range(2):
                        dv0 = emit_dequant(
                            dpk, dsc, (2 * hp + i2) * 128,
                            gidx * 512, gidx * 32, slot=i2,
                            out_e=WDT[:, 0, 4 * wd:4 * wd + 4,
                                      i2 * 128:(i2 + 1) * 128],
                            out_o=WDT[:, 1, 4 * wd:4 * wd + 4,
                                      i2 * 128:(i2 + 1) * 128],
                            pk_free=ev_pk_free[i2],
                            ab_free=ev_ab_free[i2],
                            wnat_free=ev_wnat_free[i2],
                            xbar_free=ev_wdt_free[wd],
                            pkw=512)
                        ev_pk_free[i2] = dv0["pk_next"]
                        ev_ab_free[i2] = dv0["ab_next"]
                        ev_wnat_free[i2] = dv0["wnat_next"]
                        dsp = mw(dsp, dv0["xb"])
                    evw = E("sp", lambda e, g_=gidx, h_=hp, w=wd:
                        e.dma_start(out=wspd[g_, h_],
                                    in_=WDT[:, :, 4 * w:4 * w + 4, :]),
                        waits=dsp, dma=True, sem="sdw%d" % wd)
                    return mw(dsp, {"sdw%d" % wd: evw})

                deqA = deq_gateup(0)
                last_evi = 0
                for Bb in range(NB):
                    gidx = Bb // 4
                    deq = deqA
                    wslot = {0: (2 * Bb) % 4, 1: (2 * Bb + 1) % 4}
                    dspA = None
                    if Bb % 4 == 3:
                        dspA = deq_down(gidx, 0)
                    if Bb + 1 < NB:
                        deqA = deq_gateup(Bb + 1)
                    ic0 = 2 * (Bb % 4)
                    ev_gmm = ev_umm = 0
                    for par in range(2):
                        for mi in range(2):
                            for ts in range(2):
                                bank = mi * 2 + ts
                                for kk in range(16):
                                    evm = E("pe", lambda e, k=kk,
                                            p=par, t=ts, b=bank, tt=th,
                                            w=wslot[mi]:
                                            e.matmul(
                                            out=pp[b][:],
                                            lhsT=WGT[:, w, k, p:256:2],
                                            rhs=H1T[:, k,
                                                tt * 1024 + t * 512:
                                                tt * 1024 + (t + 1) * 512],
                                            start=(k == 0), stop=(k == 15)),
                                        waits=mw(
                                            deq[mi], {"th1": ev_h1t},
                                            ev_psum_free[bank]))
                                if mi == 0:
                                    ev_gmm = evm
                                else:
                                    ev_umm = evm
                        for ts in range(2):
                            gb, ub = ts, 2 + ts
                            evg = E("act", lambda e, b=gb, t=ts:
                                e.activation(
                                    out=G1B[:, t, :], in_=pp[b][:],
                                    func=AF.Gelu_apprx_tanh),
                                waits=mw({"pe": ev_gmm}, ev_g1_free[ts]))
                            evi = E("dve", lambda e, b=ub, t=ts, p=par,
                                    i0=ic0: e.tensor_tensor(
                                    out=INTER[:, i0 + p,
                                              t * 512:(t + 1) * 512],
                                    in0=G1B[:, t, :], in1=pp[b][:],
                                    op=OP.mult),
                                waits=mw({"act": evg, "pe": ev_umm},
                                         ev_inter_free))
                            last_evi = evi
                            ev_g1_free[ts] = {"dve": evi}
                            ev_psum_free[gb] = {"act": evg}
                            ev_psum_free[ub] = {"dve": evi}
                    ev_wgt_free[wslot[0]] = mw({"pe": ev_gmm}, deq[0])
                    ev_wgt_free[wslot[1]] = mw({"pe": ev_umm}, deq[1])
                    if Bb % 4 == 3:
                        ev_int_dve = last_evi
                        for hp in range(8):
                            wd = hp % 2
                            dsp = dspA
                            if hp + 1 < 8:
                                dspA = deq_down(gidx, hp + 1)
                            for par2 in range(2):
                                for ts in range(2):
                                    db = 4 + ts
                                    for c8 in range(4):
                                        for par in range(2):
                                            evm = E("pe", lambda e, c=c8,
                                                    p=par, p2=par2, t=ts,
                                                    b=db, w=wd: e.matmul(
                                                    out=pp[b][:],
                                                    lhsT=WDT[:, p,
                                                             4 * w + c,
                                                             p2:256:2],
                                                    rhs=INTER[:, 2 * c + p,
                                                        t * 512:
                                                        (t + 1) * 512],
                                                    start=(c == 0 and
                                                           p == 0),
                                                    stop=(c == 3 and
                                                          p == 1)),
                                                waits=mw(
                                                    dsp,
                                                    {"dve": ev_int_dve},
                                                    ev_psum_free[db]))
                                    if gidx == 0:
                                        eva = E("dve", lambda e, b=db,
                                                h2=hp, p2=par2, t=ts:
                                            e.tensor_copy(
                                                out=ACC[:, 2 * h2 + p2,
                                                    t * 512:(t + 1) * 512],
                                                in_=pp[b][:]),
                                            waits=mw({"pe": evm},
                                                     ev_acc_free,
                                                     {"th1": ev_h1t},
                                                     ev_psum_free[db]))
                                    else:
                                        eva = E("dve", lambda e, b=db,
                                                h2=hp, p2=par2, t=ts:
                                            e.tensor_tensor(
                                                out=ACC[:, 2 * h2 + p2,
                                                    t * 512:(t + 1) * 512],
                                                in0=pp[b][:],
                                                in1=ACC[:, 2 * h2 + p2,
                                                    t * 512:(t + 1) * 512],
                                                op=OP.add),
                                            waits=mw({"pe": evm},
                                                     ev_psum_free[db]))
                                    ev_psum_free[db] = {"dve": eva}
                            ev_wdt_free[wd] = mw({"pe": em.n["pe"]}, dsp)
                        ev_inter_free = {"pe": em.n["pe"]}
                ev_dn = em.n["dve"]
                ev_pe_dn = em.n["pe"]
                ev_ffnr_free = {}
                ev_h1r_free = {}
                ev_z3_free = {"pe": ev_pe_dn}
                ev_outs_free = {}
                ev_accbf_free = {}
                for jj in range(8):
                    jt = th * 8 + jj
                    evc = E("dve", lambda e, j=jj: e.tensor_copy(
                        out=ACCBF[:, :, :],
                        in_=ACC[:, :, j * 128:(j + 1) * 128]),
                        waits=mw({"dve": ev_dn}, ev_accbf_free))
                    evf = E("sp", lambda e: e.dma_start_transpose(
                        out=FFNR.rearrange("p (c t) -> p c t", c=16, t=128),
                        in_=ACCBF[:, :, :]),
                        waits=mw({"dve": evc, "pe": ev_pe_dn},
                                 ev_ffnr_free), dma=True, sem="trc")
                    ev_accbf_free = {"trc": evf}
                    evc2 = E("dve", lambda e, t=jt: e.tensor_copy(
                        out=ACCBF[:, :, :],
                        in_=H1T[:, :, t * 128:(t + 1) * 128]),
                        waits=mw({"trc": evf}, ev_accbf_free))
                    evh = E("sp", lambda e: e.dma_start_transpose(
                        out=H1R.rearrange("p (c t) -> p c t", c=16, t=128),
                        in_=ACCBF[:, :, :]),
                        waits=mw({"dve": evc2, "pe": ev_pe_dn},
                                 ev_h1r_free), dma=True, sem="trc")
                    ev_accbf_free = {"trc": evh}
                    evz = E("dve", lambda e: e.tensor_tensor(
                        out=Z3, in0=H1R, in1=FFNR, op=OP.add),
                        waits=mw({"trc": evh}, ev_z3_free))
                    ev_ffnr_free = {"dve": evz}
                    ev1 = E("dve", lambda e: e.tensor_reduce(
                        out=stat(S_SUM), in_=Z3, axis=AX, op=OP.add),
                        waits={"dve": evz})
                    ev2 = E("dve", lambda e: e.tensor_scalar(
                        out=stat(S_NM), in0=stat(S_SUM), scalar1=-1.0 / D,
                        scalar2=None, op0=OP.mult), waits={"dve": ev1})
                    ev3 = E("act", lambda e: e.activation(
                        out=H1R, in_=Z3, func=AF.Square, bias=stat(S_NM),
                        accum_out=stat(S_SSQ)),
                        waits={"dve": ev2, "trc": evh})
                    ev_h1r_free = {"act": ev3, "dve": ev1}
                    ev4 = E("act", lambda e: e.activation(
                        out=stat(S_LNV), in_=stat(S_SSQ), func=AF.Ln,
                        bias=stat(S_EPS), scale=1.0 / D),
                        waits={"act": ev3})
                    ev5 = E("act", lambda e: e.activation(
                        out=stat(S_RSTD), in_=stat(S_LNV), func=AF.Exp,
                        scale=-0.5), waits={"act": ev4})
                    ev6 = E("dve", lambda e: e.tensor_tensor(
                        out=stat(S_NMRS), in0=stat(S_NM),
                        in1=stat(S_RSTD), op=OP.mult),
                        waits={"act": ev5, "dve": ev2})
                    outp = OUTS.rearrange("p (b s h) -> p b h s",
                                          b=8, s=128, h=2)
                    ev7 = E("act", lambda e, o=outp: e.activation(
                        out=o, in_=Z3, func=AF.Identity,
                        scale=stat(S_RSTD), bias=stat(S_NMRS)),
                        waits=mw({"dve": ev6}, ev_outs_free))
                    ev_z3_free = {"act": ev7}
                    ev_st = E("sp", lambda e, t=jt: e.dma_start(
                        out=y_hbm[t * 128:(t + 1) * 128, :], in_=OUTS),
                        waits={"act": ev7}, dma=True, sem="st")
                    ev_outs_free = {"st": ev_st}
                ev_acc_free = {"dve": em.n["dve"]}
                ln3r = {"st": em.n["st"], "act": em.n["act"],
                        "dve": em.n["dve"], "trc": em.n["trc"]}
                ev_inter_free = mw(ev_inter_free, ln3r)
                ev_wgt_free = [mw(w, ln3r) for w in ev_wgt_free]
                ev_wdt_free = [mw(w, ln3r) for w in ev_wdt_free]
                ev_pk_free = [mw(w, ln3r) for w in ev_pk_free]
                ev_ab_free = [mw(w, ln3r) for w in ev_ab_free]

        E("sp", lambda e: e.nop(),
          waits={s: em.n[s] for s in SEMS}, noinc=True, sem="st")

        CSEMS = ("pe", "act", "dve")
        waited = {s: set() for s in CSEMS}
        for eng in ENGS:
            for w2, _, _, _ in em.q[eng]:
                for d, v in w2:
                    if d in CSEMS:
                        waited[d].add(v)
        rank = {s: {} for s in CSEMS}
        for s in CSEMS:
            for i, v in enumerate(sorted(waited[s])):
                rank[s][v] = i + 1
        carries = {s: set(rank[s]) for s in CSEMS}

        def emit(eng):
            cum = {s: 0 for s in CSEMS}

            def run(engobj):
                for w2, builder, inc, sem in em.q[eng]:
                    for d, v in w2:
                        if d in CSEMS:
                            engobj.wait_ge(sems[d], rank[d][v])
                        else:
                            engobj.wait_ge(sems[d], v)
                    inst = builder(engobj)
                    if inst is None or inc == 0:
                        continue
                    if sem in CSEMS:
                        cum[sem] += inc
                        if cum[sem] in carries[sem]:
                            inst.then_inc(sems[sem], 1)
                    else:
                        inst.then_inc(sems[sem], inc)
            return run

        block.tensor(emit("pe"))
        block.scalar(emit("act"))
        block.vector(emit("dve"))
        block.sync(emit("sp"))
        block.gpsimd(emit("pool"))

    return nc


def _host_layout(inputs):
    f32 = np.float32

    def rT(name):
        return np.ascontiguousarray(
            np.asarray(inputs[name], f32).reshape(16, 128).T)

    com = {
        "wq": np.ascontiguousarray(np.asarray(inputs["L0_attn_Wq"], f32)),
        "wk": np.ascontiguousarray(np.asarray(inputs["L0_attn_Wk"], f32)),
        "wv": np.ascontiguousarray(np.asarray(inputs["L0_attn_Wv"], f32)),
        "wo": np.ascontiguousarray(np.asarray(inputs["L0_attn_Wo"], f32)),
        "gT": rT("L0_attn_ln_g"), "bT": rT("L0_attn_ln_b"),
        "bqT": rT("L0_attn_bq"), "bkT": rT("L0_attn_bk"),
        "bvT": rT("L0_attn_bv"), "boT": rT("L0_attn_bo"),
        "gpk": np.ascontiguousarray(
            np.asarray(inputs["L0_ffn_gate_W_packed"], np.int32)),
        "upk": np.ascontiguousarray(
            np.asarray(inputs["L0_ffn_up_W_packed"], np.int32)),
        "dpk": np.ascontiguousarray(
            np.asarray(inputs["L0_ffn_down_W_packed"], np.int32)),
        "gsc": np.ascontiguousarray(
            np.asarray(inputs["L0_ffn_gate_scales"], f32)),
        "usc": np.ascontiguousarray(
            np.asarray(inputs["L0_ffn_up_scales"], f32)),
        "dsc": np.ascontiguousarray(
            np.asarray(inputs["L0_ffn_down_scales"], f32)),
    }
    q = np.arange(128)[:, None]
    k = np.arange(128)[None, :]
    com["mask"] = np.where(k <= q, 0.0, NEGB).astype(f32)
    return com


def kernel(**inputs):
    X = np.ascontiguousarray(np.asarray(inputs["L0_attn_X"], np.float32))
    com = _host_layout(inputs)
    nc = build_nc()
    in_maps = [dict(com, x=np.ascontiguousarray(X[b])) for b in range(8)]
    res = run_bass_kernel_spmd(nc, in_maps, core_ids=list(range(8)))
    return np.stack([np.asarray(res.results[b]["y"]) for b in range(8)])
